# version 13
# baseline (speedup 1.0000x reference)
import numpy as np
import ml_dtypes
import concourse.bass as bass
import concourse.mybir as mybir
from concourse.bass import AP
from concourse.bass_utils import run_bass_kernel_spmd

F32 = mybir.dt.float32
F16 = mybir.dt.float16
BF16 = mybir.dt.bfloat16
U8 = mybir.dt.uint8
AF = mybir.ActivationFunctionType
ALU = mybir.AluOpType
X = mybir.AxisListType.X

NF4 = np.array([-1.0, -0.6961928009986877, -0.5250730514526367, -0.39491748809814453,
                -0.28444138169288635, -0.18477343022823334, -0.09105003625154495, 0.0,
                0.07958029955625534, 0.16093020141124725, 0.24611230194568634,
                0.33791524171829224, 0.44070982933044434, 0.5626170039176941,
                0.7229568362236023, 1.0], dtype=np.float64)

NCORE = 8
RPC = 1024
WAVE = 512
NW = RPC // WAVE
NT = 32
OSL = 512
PC0, PS0 = 28.25, 28.0
INV128 = 1.0 / 128.0
AC_M75, AC_LN2, AC_NEG1, AC_NEGH, AC_EPS, AC_I4096, AC_HALF, AC_128 = range(8)


def nf4_coeffs():
    k = np.arange(16, dtype=np.float64)
    z = k - 7.5
    u = z * z
    uv = np.unique(u)
    even = 0.5 * (NF4 + NF4[::-1])
    odd = 0.5 * (NF4 - NF4[::-1]) / z
    idx = [np.where(u == v)[0][0] for v in uv]
    w = (uv - PC0) / PS0
    V = np.vander(w, 8, increasing=True)
    return ([float(v) for v in np.linalg.solve(V, even[idx])],
            [float(v) for v in np.linalg.solve(V, odd[idx])])


CE, CO = nf4_coeffs()

TSLOTS = []
TSLOT_OF = {}
for d in range(8):
    for a in range(4):
        for b in range(4):
            if d == 0 and b > a:
                continue
            TSLOT_OF[(d, a, b)] = len(TSLOTS)
            TSLOTS.append((d, a, b))
assert len(TSLOTS) == 122


def host_constants():
    f = np.float32
    c = {}
    jm = np.zeros((128, 128), np.float16)
    jm[np.arange(128), 127 - np.arange(128)] = 1.0
    c["jmat"] = jm
    c["ones16"] = np.ones((128, 128), np.float16)
    c["onescol16"] = np.ones((128, 1), np.float16)
    c["onescolbf"] = np.ones((128, 1), ml_dtypes.bfloat16)
    c["ident32"] = np.eye(128, dtype=f)
    ang64 = 2 * np.pi * np.outer(np.arange(64), np.arange(64)) / 64
    e64c = np.cos(ang64)
    e64s = np.sin(ang64)

    def pad(m, scale=1.0):
        out = np.zeros((128, 64), f)
        out[:64, :] = (m * scale).astype(f)
        return out

    c["e64c_p"] = pad(e64c)
    c["e64s_p"] = pad(e64s)
    c["e64sn_p"] = pad(-e64s)
    c["e64cL_p"] = pad(e64c, 1.0 / 4096.0)
    c["e64snL_p"] = pad(-e64s, 1.0 / 4096.0)
    angT = 2 * np.pi * np.outer(np.arange(64), np.arange(64)) / 4096
    c["twc"] = np.cos(angT).astype(f)
    c["tws"] = np.sin(angT).astype(f)
    t = np.tan(np.pi * np.arange(4096) / 4096)
    c["t64"] = t.reshape(64, 64).astype(f)
    c["tbc"] = np.tile(t.astype(f)[None, :], (64, 1))
    ac = np.zeros((128, 8), f)
    ac[:, AC_M75] = -7.5
    ac[:, AC_LN2] = np.log(2.0)
    ac[:, AC_NEG1] = -1.0
    ac[:, AC_NEGH] = -0.5
    ac[:, AC_EPS] = 1e-5
    ac[:, AC_I4096] = 1.0 / 4096.0
    ac[:, AC_HALF] = 0.5
    ac[:, AC_128] = 128.0
    c["actc"] = ac
    return c


def stage_inputs(inputs):
    consts = host_constants()
    x = np.ascontiguousarray(inputs["L0_attn_u"].reshape(8192, 4096).T)
    codes_u8 = np.ascontiguousarray(inputs["L0_ffn_w_codes"].view(np.uint8)[:, 0::4])
    codesT = np.ascontiguousarray(codes_u8.T)
    amB = np.repeat(np.ascontiguousarray(inputs["L0_ffn_absmax"].T), 64, axis=0)

    def omaj(cs):
        t = cs.reshape(4, 8, 128, 4, 128).transpose(3, 0, 2, 1, 4)
        return np.ascontiguousarray(t.reshape(2048, 1024))
    small8 = np.stack([
        inputs["L0_attn_Lambda_ri"][:, 0], inputs["L0_attn_Lambda_ri"][:, 1],
        inputs["L0_attn_p_ri"][:, 0], inputs["L0_attn_p_ri"][:, 1],
        inputs["L0_attn_q_ri"][:, 0], inputs["L0_attn_q_ri"][:, 1],
        inputs["L0_attn_Ct_ri"][0, :, 0], inputs["L0_attn_Ct_ri"][0, :, 1],
    ], axis=1).astype(np.float32)
    in_maps = []
    for cid in range(NCORE):
        m = dict(consts)
        m["xT"] = np.ascontiguousarray(x[:, cid * RPC:(cid + 1) * RPC])
        m["codesT"] = omaj(codesT[:, cid * OSL:(cid + 1) * OSL])
        m["amB"] = omaj(amB[:, cid * OSL:(cid + 1) * OSL].astype(np.float32))
        m["waf"] = np.ascontiguousarray(inputs["L0_ffn_W_a"].astype(np.float32))
        m["wbf"] = np.ascontiguousarray(inputs["L0_ffn_W_b"].astype(np.float32))
        m["small8"] = small8
        m["vcre"] = np.ascontiguousarray(inputs["L0_attn_Vc_ri"][:, :, 0]).astype(np.float32)
        m["vcim"] = np.ascontiguousarray(inputs["L0_attn_Vc_ri"][:, :, 1]).astype(np.float32)
        m["bpb"] = np.tile(inputs["L0_attn_Bp"].reshape(1, 64), (64, 1)).astype(np.float32)
        m["dls"] = np.array([[float(inputs["L0_attn_D"][0]),
                              float(inputs["L0_attn_log_step"][0])]], dtype=np.float32)
        in_maps.append(m)
    return in_maps


class Ctr:
    def __init__(self):
        self.c = {}

    def inc(self, name, n=1):
        self.c[name] = self.c.get(name, 0) + n
        return self.c[name]

    def val(self, name):
        return self.c.get(name, 0)


def build_program(stop_after=None, dbg=()):
    nc = bass.Bass("TRN2", target_bir_lowering=False, debug=False)
    ctr = Ctr()

    din = lambda n, s, d: nc.dram_tensor(n, s, d, kind="ExternalInput").ap()
    xT = din("xT", [4096, RPC], F32)
    codesT = din("codesT", [2048, 1024], U8)
    amB = din("amB", [2048, 1024], F32)
    waf = din("waf", [4096, 16], F32)
    wbf = din("wbf", [16, 4096], F32)
    small8 = din("small8", [64, 8], F32)
    vcre = din("vcre", [64, 64], F32)
    vcim = din("vcim", [64, 64], F32)
    bpb = din("bpb", [64, 64], F32)
    dls = din("dls", [1, 2], F32)
    jmat = din("jmat", [128, 128], F16)
    ones16d = din("ones16", [128, 128], F16)
    onescol16 = din("onescol16", [128, 1], F16)
    onescolbf = din("onescolbf", [128, 1], BF16)
    ident32d = din("ident32", [128, 128], F32)
    e64c_p = din("e64c_p", [128, 64], F32)
    e64s_p = din("e64s_p", [128, 64], F32)
    e64sn_p = din("e64sn_p", [128, 64], F32)
    e64cL_p = din("e64cL_p", [128, 64], F32)
    e64snL_p = din("e64snL_p", [128, 64], F32)
    twc_d = din("twc", [64, 64], F32)
    tws_d = din("tws", [64, 64], F32)
    t64_d = din("t64", [64, 64], F32)
    tbc_d = din("tbc", [64, 4096], F32)
    actc_d = din("actc", [128, 8], F32)

    outT = nc.dram_tensor("outT", [4096, RPC], F32, kind="ExternalOutput").ap()

    wslice = nc.dram_tensor("wslice", [4 * NT * 128 * 64], F32, kind="Internal")
    wfull = nc.dram_tensor("wfull", [32 * NT * 128 * 64], F32, kind="Internal",
                           addr_space="Shared")
    kpad = nc.dram_tensor("kpad", [4864], F16, kind="Internal")
    krowsd = nc.dram_tensor("krowsd", [8 * 4096], F32, kind="Internal")

    dbg_t = {}
    if "k" in dbg:
        dbg_t["dbg_k"] = nc.dram_tensor("dbg_k", [4864], F16, kind="ExternalOutput").ap()
        dbg_t["dbg_t0"] = nc.dram_tensor("dbg_t0", [128, 128], F16, kind="ExternalOutput").ap()
        dbg_t["dbg_ar"] = nc.dram_tensor("dbg_ar", [128, 128], F32, kind="ExternalOutput").ap()
        dbg_t["dbg_cau"] = nc.dram_tensor("dbg_cau", [64, 512], F32, kind="ExternalOutput").ap()
        dbg_t["dbg_vt"] = nc.dram_tensor("dbg_vt", [128, 8], F32, kind="ExternalOutput").ap()
        dbg_t["dbg_kc"] = nc.dram_tensor("dbg_kc", [64, 16], F32, kind="ExternalOutput").ap()
        dbg_t["dbg_a2", ] = nc.dram_tensor("dbg_a2", [128, 4], F32, kind="ExternalOutput").ap()
        dbg_t["dbg_kinv"] = nc.dram_tensor("dbg_kinv", [128, 512], F32, kind="ExternalOutput").ap()
    if "x" in dbg:
        dbg_t["dbg_x"] = nc.dram_tensor("dbg_x", [4096, RPC], F16, kind="ExternalOutput").ap()
    if "w" in dbg:
        dbg_t["dbg_w"] = nc.dram_tensor("dbg_w", [4 * NT * 128 * 64], F32,
                                        kind="ExternalOutput").ap()

    from contextlib import ExitStack
    es = ExitStack()
    sbt = lambda name, shape, dtype: es.enter_context(nc.sbuf_tensor(name, shape, dtype))
    pst = lambda name, shape: es.enter_context(nc.psum_tensor(name, shape, F32))

    x16 = sbt("x16", [128, NT * RPC], F16)
    xc16 = sbt("xc16", [128, NT * WAVE], F16)
    t16 = sbt("t16", [128, 56 * 128], F16)
    wb16 = sbt("wb16", [128, 4096], F16)
    wa16 = sbt("wa16", [128, NT * 16], F16)
    wbuf = sbt("wbuf", [128, 3 * 2048], F16)
    traw16 = sbt("traw16", [128, WAVE], F16)
    mbc = sbt("mbc", [128, WAVE], F16)
    sbc = sbt("sbc", [128, WAVE], F16)
    mbc1 = sbt("mbc1", [128, WAVE], F16)
    sbc1 = sbt("sbc1", [128, WAVE], F16)
    rp = sbt("rp", [128, 2 * WAVE], F16)
    cod = sbt("cod", [128, 2 * 1024], U8)
    ambt = sbt("ambt", [128, 2 * 1024], F16)
    scr = sbt("scr", [128, 18 * 512], F32)
    wout = sbt("wout", [128, 2 * 1024], F16)
    xsq = sbt("xsq", [128, 2 * 512], BF16)
    raws = sbt("raws", [128, 2 * 512], F16)
    cbuf = sbt("cbuf", [128, 2 * 512], F16)
    outst = sbt("outst", [128, 2 * 512], F32)
    jm16 = sbt("jm16", [128, 128], F16)
    idd16 = sbt("idd16", [128, 128], F16)
    on16 = sbt("on16", [128, 128], F16)
    onc16 = sbt("onc16", [128, 1], F16)
    oncbf = sbt("oncbf", [128, 1], BF16)
    id32 = sbt("id32", [128, 128], F32)
    e64c_s = sbt("e64c_s", [128, 64], F32)
    e64s_s = sbt("e64s_s", [128, 64], F32)
    e64sn_s = sbt("e64sn_s", [128, 64], F32)
    e64cL_s = sbt("e64cL_s", [128, 64], F32)
    e64snL_s = sbt("e64snL_s", [128, 64], F32)
    twc_s = sbt("twc_s", [64, 64], F32)
    tws_s = sbt("tws_s", [64, 64], F32)
    t64_s = sbt("t64_s", [64, 64], F32)
    actc_s = sbt("actc_s", [128, 8], F32)
    sm8 = sbt("sm8", [64, 8], F32)
    vcre_s = sbt("vcre_s", [64, 64], F32)
    vcim_s = sbt("vcim_s", [64, 64], F32)
    bpb_s = sbt("bpb_s", [64, 64], F32)
    dls_s = sbt("dls_s", [1, 2], F32)
    kcols = sbt("kcols", [64, 16], F32)
    vtile = sbt("vtile", [128, 8], F32)
    acol = sbt("acol", [128, 4], F32)
    a2D = sbt("a2D", [128, 4], F32)
    ones32 = sbt("ones32", [128, 128], F32)
    rowsc = sbt("rowsc", [1, 16], F32)
    arre = sbt("arre", [128, 64], F32)
    arim = sbt("arim", [128, 64], F32)
    idt = sbt("idt", [128, 3 * 64], F32)
    idtT = sbt("idtT", [128, 2 * 64], F32)
    k16t = sbt("k16t", [64, 64], F16)
    zrow = sbt("zrow", [1, 511], F16)
    han = sbt("han", [128, 2 * 896], F16)

    pA = pst("pA", [128, 512])
    pB = pst("pB", [128, 512])
    pC = pst("pC", [128, 512])
    pStat = pst("pStat", [128, 1024])
    pD = pst("pD", [128, 512])
    pE = pst("pE", [128, 512])

    sems = {n: es.enter_context(nc.semaphore(n))
            for n in ["lc", "lx", "ld0", "ld1", "lgw0", "lgw1", "lt", "lp",
                      "lh0", "lh1", "lw0", "lw1", "lw2", "lk", "lkst",
                      "lo0", "lo1", "ldb", "la0", "la1", "pe", "dv", "ac", "cc"]}
    block = es.enter_context(nc.Block())

    def W(eng, name, val=None):
        eng.wait_ge(sems[name], ctr.val(name) if val is None else val)

    def I(instr, name, n=1):
        instr.then_inc(sems[name], n)
        return ctr.inc(name, n)

    def fence_dv(v):
        W(v, "dv")
        return I(v.tensor_copy(rowsc[0:1, 8:9], rowsc[0:1, 9:10]), "dv")

    def fence_ac(s):
        W(s, "ac")
        return I(s.activation(rowsc[0:1, 10:11], rowsc[0:1, 11:12], AF.Copy), "ac")

    def xt(i, w=None):
        if w is None:
            return x16[:, i * RPC:(i + 1) * RPC]
        return x16[:, i * RPC + w * WAVE: i * RPC + (w + 1) * WAVE]

    def xct(i):
        return xc16[:, i * WAVE:(i + 1) * WAVE]

    wsl_t = wslice.ap().tensor
    wfull_t = wfull.ap().tensor
    kpad_t = kpad.ap().tensor
    krows_t = krowsd.ap().tensor

    with es:
        @block.sync
        def _(sync):
            for dst, src in [(jm16, jmat), (on16, ones16d), (onc16, onescol16),
                             (oncbf, onescolbf), (id32, ident32d),
                             (e64c_s, e64c_p), (e64s_s, e64s_p), (e64sn_s, e64sn_p),
                             (e64cL_s, e64cL_p), (e64snL_s, e64snL_p),
                             (twc_s, twc_d), (tws_s, tws_d), (t64_s, t64_d),
                             (actc_s, actc_d), (sm8, small8), (vcre_s, vcre),
                             (vcim_s, vcim), (bpb_s, bpb), (dls_s, dls)]:
                I(sync.dma_start(out=dst[:, :], in_=src), "lc", 16)
        LS_CONST = ctr.val("lc")

        deq = {}
        AGQ = {}
        ffn1_mile = {}

        def deq_load(k):
            kbi = k % 2

            @block.gpsimd
            def _(g, k=k, kbi=kbi):
                if k >= 2:
                    W(g, "ac", deq[("cast", k - 2)])
                    W(g, "dv", deq[("am", k - 2)])
                srcc = AP(codesT.tensor, k * 131072, [[1024, 128], [1, 1024]])
                I(g.dma_start(out=cod[:, kbi * 1024:(kbi + 1) * 1024], in_=srcc),
                  "ld%d" % kbi, 16)
                srca = AP(amB.tensor, k * 131072, [[1024, 128], [1, 1024]])
                I(g.dma_start(out=ambt[:, kbi * 1024:(kbi + 1) * 1024], in_=srca),
                  "la%d" % kbi, 16)
            deq[("ld", k)] = ctr.val("ld%d" % kbi)
            deq[("la", k)] = ctr.val("la%d" % kbi)
        deq_load(0)
        deq_load(1)

        @block.gpsimd
        def _(g):
            for i in range(NT):
                I(g.dma_start(out=xt(i), in_=xT[i * 128:(i + 1) * 128, :]), "lx", 16)
            for i in range(NT):
                I(g.dma_start(out=wa16[:, i * 16:(i + 1) * 16],
                              in_=waf[i * 128:(i + 1) * 128, :]), "lx", 16)
        LG_X = ctr.val("lx")

        @block.vector
        def _(v):
            W(v, "lc", LS_CONST)
            for buf in [rp, arre, arim, idtT, traw16, wb16, zrow, acol, idt]:
                W(v, "dv")
                I(v.memset(buf[:, :], 0.0), "dv")
            W(v, "dv")
            I(v.memset(ones32[:, :], 1.0), "dv")
            fence_dv(v)
        DV_MEM = ctr.val("dv")

        @block.gpsimd
        def _(g):
            W(g, "dv", DV_MEM)
            I(g.dma_start(out=wb16[0:16, :], in_=wbf), "lx", 16)
        LG_WB = ctr.val("lx")

        @block.scalar
        def _(s):
            import concourse.bass_isa as bass_isa
            s.add_instruction(mybir.InstLoadActFuncSet(
                name=nc.get_next_instruction_name() if hasattr(nc, "get_next_instruction_name") else "I-preload",
                act_func_set_id=6, ins=[], outs=[]))
            W(s, "lc", LS_CONST)
            W(s, "ac")
            I(s.activation(rowsc[0:1, 0:1], dls_s[0:1, 1:2], AF.Exp,
                           bias=actc_s[0:1, AC_LN2:AC_LN2 + 1],
                           scale=actc_s[0:1, AC_NEG1:AC_NEG1 + 1]), "ac")
            fence_ac(s)
        AC_A2 = ctr.val("ac")

        @block.vector
        def _(v):
            W(v, "dv", DV_MEM)
            W(v, "ac", AC_A2)
            W(v, "dv")
            I(v.tensor_copy(acol[0:1, 0:1], rowsc[0:1, 0:1]), "dv")
            W(v, "dv")
            I(v.tensor_copy(acol[0:1, 1:2], dls_s[0:1, 0:1]), "dv")
            fence_dv(v)
        DV_ACOL = ctr.val("dv")

        @block.tensor
        def _(t):
            W(t, "dv", DV_ACOL)
            I(t.matmul(pC[:, 0:4], ones32[:, :], acol[:, 0:4],
                       start=True, stop=True), "pe")
        PE_A2 = ctr.val("pe")

        @block.vector
        def _(v):
            W(v, "pe", PE_A2)
            W(v, "dv")
            I(v.tensor_copy(a2D[:, :], pC[:, 0:4]), "dv")
            W(v, "dv")
            I(v.tensor_mul(scr[0:64, 0:64], vcre_s[:, :], bpb_s[:, :]), "dv")
            W(v, "dv")
            I(v.reduce_sum(kcols[:, 0:1], scr[0:64, 0:64], axis=X), "dv")
            W(v, "dv")
            I(v.tensor_mul(scr[0:64, 0:64], vcim_s[:, :], bpb_s[:, :]), "dv")
            W(v, "dv")
            I(v.reduce_sum(kcols[:, 1:2], scr[0:64, 0:64], axis=X), "dv")
            ctre, ctim = sm8[:, 6:7], sm8[:, 7:8]
            pre_, pim_ = sm8[:, 2:3], sm8[:, 3:4]
            qre_, qim_ = sm8[:, 4:5], sm8[:, 5:6]
            bre, bim = kcols[:, 0:1], kcols[:, 1:2]
            nctim = kcols[:, 14:15]
            nqim = kcols[:, 13:14]
            W(v, "dv")
            I(v.tensor_scalar(nctim, ctim, -1.0, None, ALU.mult), "dv")
            W(v, "dv")
            I(v.tensor_scalar(nqim, qim_, -1.0, None, ALU.mult), "dv")
            tmpc = kcols[:, 15:16]

            def cmulcols(ar_, ai_, br_, bi_, o_re, o_im):
                W(v, "dv")
                I(v.tensor_mul(o_re, ar_, br_), "dv")
                W(v, "dv")
                I(v.tensor_mul(tmpc, ai_, bi_), "dv")
                W(v, "dv")
                I(v.tensor_sub(o_re, o_re, tmpc), "dv")
                W(v, "dv")
                I(v.tensor_mul(o_im, ar_, bi_), "dv")
                W(v, "dv")
                I(v.tensor_mul(tmpc, ai_, br_), "dv")
                W(v, "dv")
                I(v.tensor_add(o_im, o_im, tmpc), "dv")

            cmulcols(ctre, nctim, bre, bim, kcols[:, 2:3], kcols[:, 3:4])
            cmulcols(ctre, nctim, pre_, pim_, kcols[:, 4:5], kcols[:, 5:6])
            cmulcols(qre_, nqim, bre, bim, kcols[:, 6:7], kcols[:, 7:8])
            cmulcols(qre_, nqim, pre_, pim_, kcols[:, 8:9], kcols[:, 9:10])
            for j, (rc, ic) in enumerate([(2, 3), (4, 5), (6, 7), (8, 9)]):
                vre = kcols[:, rc:rc + 1]
                vim = kcols[:, ic:ic + 1]
                W(v, "dv")
                I(v.tensor_copy(vtile[0:64, 2 * j:2 * j + 1], vre), "dv")
                W(v, "dv")
                I(v.tensor_scalar(vtile[64:128, 2 * j:2 * j + 1], vim, -1.0,
                                  None, ALU.mult), "dv")
                W(v, "dv")
                I(v.tensor_copy(vtile[0:64, 2 * j + 1:2 * j + 2], vim), "dv")
                W(v, "dv")
                I(v.tensor_copy(vtile[64:128, 2 * j + 1:2 * j + 2], vre), "dv")
            fence_dv(v)
        DV_VT = ctr.val("dv")

        @block.scalar
        def _(s):
            W(s, "dv", DV_VT)
            W(s, "ac")
            I(s.activation(idd16[:, :], id32[:, :], AF.Copy,
                           scale=a2D[:, 1:2]), "ac")
            fence_ac(s)
        AC_IDD = ctr.val("ac")

        def stats_and_xc(w, prev_dv, need_lg=None, prev_c=None, ms_only=False):
            st = {}
            mb_out = mbc1 if ms_only else mbc
            sb_out = sbc1 if ms_only else sbc

            @block.tensor
            def _(t):
                if prev_c is None:
                    W(t, "dv", prev_dv)
                if need_lg is not None:
                    W(t, "lx", need_lg)
                for i in range(NT):
                    if prev_c is not None:
                        W(t, "dv", prev_c[i])
                    I(t.matmul(pStat[0:1, 0:512], onc16[:, :], xt(i, w),
                               start=(i == 0), stop=(i == NT - 1)), "pe")

            for i in range(NT):
                bi = i % 2

                @block.scalar
                def _(s, i=i, bi=bi):
                    if i == 0:
                        if prev_c is None:
                            W(s, "dv", prev_dv)
                        if need_lg is not None:
                            W(s, "lx", need_lg)
                    if prev_c is not None:
                        W(s, "dv", prev_c[i])
                    if i >= 2:
                        W(s, "pe", st[("mm", i - 2)])
                    I(s.activation(xsq[:, bi * 512:(bi + 1) * 512], xt(i, w),
                                   AF.Square), "ac")
                    st[("sq", i)] = fence_ac(s)

                @block.tensor
                def _(t, i=i, bi=bi):
                    W(t, "ac", st[("sq", i)])
                    I(t.matmul(pStat[0:1, 512:1024], oncbf[:, :],
                               xsq[:, bi * 512:(bi + 1) * 512],
                               start=(i == 0), stop=(i == NT - 1)), "pe")
                    st[("mm", i)] = ctr.val("pe")
            PE_ST = ctr.val("pe")

            mrow = cbuf[0:1, 0:1024].bitcast(F32)
            qrow = xsq[0:1, 0:1024].bitcast(F32)
            mm2 = raws[0:1, 0:1024].bitcast(F32)

            @block.vector
            def _(v):
                W(v, "pe", PE_ST)
                W(v, "dv")
                I(v.tensor_scalar(mrow, pStat[0:1, 0:512], 1.0 / 4096.0, None,
                                  ALU.mult), "dv")
                W(v, "dv")
                I(v.tensor_scalar(qrow, pStat[0:1, 512:1024], 1.0 / 4096.0, None,
                                  ALU.mult), "dv")
                W(v, "dv")
                I(v.tensor_mul(mm2, mrow, mrow), "dv")
                W(v, "dv")
                I(v.tensor_sub(qrow, qrow, mm2), "dv")
                fence_dv(v)
            DV_VAR = ctr.val("dv")

            @block.scalar
            def _(s):
                W(s, "dv", DV_VAR)
                W(s, "ac")
                I(s.activation(mm2, qrow, AF.Ln,
                               bias=actc_s[0:1, AC_EPS:AC_EPS + 1]), "ac")
                W(s, "ac")
                I(s.activation(qrow, mm2, AF.Exp,
                               scale=actc_s[0:1, AC_NEGH:AC_NEGH + 1]), "ac")
                fence_ac(s)
            AC_S = ctr.val("ac")

            @block.vector
            def _(v):
                W(v, "ac", AC_S)
                W(v, "dv")
                I(v.tensor_scalar(rp[0:1, 0:512], mrow, INV128, None, ALU.mult),
                  "dv")
                W(v, "dv")
                I(v.tensor_scalar(rp[0:1, 512:1024], qrow, 128.0, None, ALU.mult),
                  "dv")
                fence_dv(v)
            DV_RP = ctr.val("dv")

            @block.tensor
            def _(t):
                W(t, "dv", DV_RP)
                I(t.matmul(pStat[:, 0:512], on16[:, :], rp[:, 0:512],
                           start=True, stop=True), "pe")
                I(t.matmul(pStat[:, 512:1024], on16[:, :], rp[:, 512:1024],
                           start=True, stop=True), "pe")
            PE_BS = ctr.val("pe")

            @block.vector
            def _(v):
                W(v, "pe", PE_BS)
                I(v.tensor_copy(mb_out[:, :], pStat[:, 0:512]), "dv")
                W(v, "dv")
                I(v.tensor_copy(sb_out[:, :], pStat[:, 512:1024]), "dv")
                if not ms_only:
                    for i in range(NT):
                        I(v.scalar_tensor_tensor(xct(i), xt(i, w), INV128,
                                                 mb_out[:, :],
                                                 ALU.mult, ALU.subtract), "dv")
                fence_dv(v)
            return ctr.val("dv")

        def lora_traw(dv_xc):
            @block.tensor
            def _(t):
                W(t, "dv", dv_xc)
                for i in range(NT):
                    I(t.matmul(pStat[0:16, 0:512], wa16[:, i * 16:(i + 1) * 16],
                               xct(i), start=(i == 0), stop=(i == NT - 1)), "pe")
            PE_TR = ctr.val("pe")

            @block.vector
            def _(v):
                W(v, "pe", PE_TR)
                I(v.tensor_copy(traw16[0:16, :], pStat[0:16, 0:512]), "dv")
                fence_dv(v)
            return ctr.val("dv")

        def ffn_slot(w, o, mile, dv_ready, final=False, prev_mm=0, prev_st=0,
                     wq="sync"):
            if True:
                bi = o % 2

                @(block.sync if wq == "sync" else block.gpsimd)
                def _(sync, o=o):
                    if o % 8 == 0 or wq != "sync":
                        W(sync, "cc", AGQ[o // 8])
                    for h2 in range(2):
                        h = 2 * o + h2
                        if h < 3:
                            W(sync, "pe", prev_mm)
                        else:
                            W(sync, "pe", mile[("mmh", h - 3)])
                        src = AP(wfull_t, o * 262144 + h2 * 1024,
                                 [[2048, 128], [1, 1024]])
                        dst = wbuf[:, (h % 3) * 2048:
                                   (h % 3 + 1) * 2048].bitcast(F32)
                        I(sync.dma_start(out=dst, in_=src), "lw%d" % (h % 3), 16)
                        mile[("ldh", h)] = ctr.val("lw%d" % (h % 3))

                @block.tensor
                def _(t, o=o, bi=bi):
                    if o == 0:
                        W(t, "dv", dv_ready)
                        W(t, "lx", LG_WB)
                    if o >= 2:
                        W(t, "ac", mile[("ev", o - 2)])
                    p = pA if bi == 0 else pC
                    for h2 in range(2):
                        h = 2 * o + h2
                        W(t, "lw%d" % (h % 3), mile[("ldh", h)])
                        hb = (h % 3) * 2048
                        for i in range(16):
                            it = h2 * 16 + i
                            mm = t.matmul(p[:, 0:512],
                                          wbuf[:, hb + i * 128:hb + (i + 1) * 128],
                                          xct(it), start=(it == 0), stop=False)
                            if i == 15 and h2 == 0:
                                I(mm, "pe")
                                mile[("mmh", h)] = ctr.val("pe")
                    ot = 4 * (o % 8) + o // 8
                    I(t.matmul(p[:, 0:512], wb16[:, ot * 128:(ot + 1) * 128],
                               traw16[:, :], start=False, stop=True), "pe")
                    mile[("mmh", 2 * o + 1)] = ctr.val("pe")
                    mile[("mm", o)] = ctr.val("pe")

                @block.scalar
                def _(s, o=o, bi=bi):
                    W(s, "pe", mile[("mm", o)])
                    if o >= 2:
                        W(s, "dv", mile[("c", o - 2)])
                    p = pA if bi == 0 else pC
                    I(s.activation(raws[:, bi * 512:(bi + 1) * 512], p[:, 0:512],
                                   AF.Copy), "ac")
                    mile[("ev", o)] = fence_ac(s)

                @block.vector
                def _(v, o=o, bi=bi):
                    W(v, "ac", mile[("ev", o)])
                    if final and o >= 2:
                        W(v, "lo%d" % bi, mile[("st", o - 2)])
                    if final and o < 2:
                        W(v, "lo%d" % bi, prev_st[bi] if prev_st else 0)
                    ot = 4 * (o % 8) + o // 8
                    rb = raws[:, bi * 512:(bi + 1) * 512]
                    cb = cbuf[:, bi * 512:(bi + 1) * 512]
                    I(v.tensor_mul(cb, rb, sbc[:, :]), "dv")
                    if not final:
                        I(v.tensor_add(xt(ot, w), cb, xt(ot, w)), "dv")
                    else:
                        ob = outst[:, bi * 512:(bi + 1) * 512]
                        I(v.tensor_add(ob, cb, xt(ot, w)), "dv")
                    mile[("c", o)] = fence_dv(v)

                if final:
                    @block.gpsimd
                    def _(g, o=o, bi=bi):
                        W(g, "dv", mile[("c", o)])
                        ob = outst[:, bi * 512:(bi + 1) * 512]
                        ot = 4 * (o % 8) + o // 8
                        I(g.dma_start(out=outT[ot * 128:(ot + 1) * 128,
                                               w * WAVE:(w + 1) * WAVE],
                                      in_=ob), "lo%d" % bi, 16)
                        mile[("st", o)] = ctr.val("lo%d" % bi)
        def ffn_end(mile, final=False):
            cft = {4 * (o % 8) + o // 8: mile[("c", o)] for o in range(NT)}
            return (ctr.val("dv"), mile[("mm", NT - 1)],
                    ((mile[("st", NT - 2)], mile[("st", NT - 1)]) if final else None),
                    cft)

        def ffn(w, dv_ready, final=False, prev_mm=0, prev_st=0):
            mile = {}
            for o in range(NT):
                ffn_slot(w, o, mile, dv_ready, final, prev_mm, prev_st)
            return ffn_end(mile, final)

        def conv2(w, dv_ready):
            mile = {}
            groups = [(ci, a) for ci in range(8) for a in range(4)]
            for gidx, (ci, a) in enumerate(groups):
                bi = gidx % 2
                ft = 4 * ci + a

                @block.tensor
                def _(t, ci=ci, a=a, gidx=gidx, bi=bi):
                    if gidx == 0:
                        W(t, "dv", dv_ready)
                        W(t, "ac", AC_T16)
                        W(t, "ac", AC_IDD)
                    if gidx >= 2:
                        W(t, "ac", mile[("ev", gidx - 2)])
                    p = pA if bi == 0 else pC
                    ft_ = 4 * ci + a
                    I(t.matmul(p[:, 0:512], idd16[:, :], xct(ft_),
                               start=True, stop=False), "pe")
                    terms = []
                    for d in range(ci, -1, -1):
                        bs = range(4) if d > 0 else range(a + 1)
                        for b_ in bs:
                            terms.append((d * 7 + (a - b_ + 3),
                                          4 * (ci - d) + b_))
                    for k_, (slot, xti) in enumerate(terms):
                        I(t.matmul(p[:, 0:512],
                                   t16[:, slot * 128:(slot + 1) * 128],
                                   xct(xti), start=False,
                                   stop=(k_ == len(terms) - 1)), "pe")
                    mile[("mm", gidx)] = ctr.val("pe")

                @block.scalar
                def _(s, gidx=gidx, bi=bi):
                    W(s, "pe", mile[("mm", gidx)])
                    if gidx >= 2:
                        W(s, "dv", mile[("c", gidx - 2)])
                    p = pA if bi == 0 else pC
                    I(s.activation(raws[:, bi * 512:(bi + 1) * 512], p[:, 0:512],
                                   AF.Copy), "ac")
                    mile[("ev", gidx)] = fence_ac(s)

                @block.vector
                def _(v, gidx=gidx, bi=bi, ft=ft):
                    W(v, "ac", mile[("ev", gidx)])
                    rb = raws[:, bi * 512:(bi + 1) * 512]
                    cb = cbuf[:, bi * 512:(bi + 1) * 512]
                    I(v.tensor_mul(cb, rb, sbc[:, :]), "dv")
                    I(v.tensor_add(xt(ft, w), cb, xt(ft, w)), "dv")
                    mile[("c", gidx)] = fence_dv(v)
            return ctr.val("dv"), {4 * ci + a: mile[("c", g2)]
                                   for g2, (ci, a) in enumerate(groups)}

        def emit_pair(k):
            kbi = k % 2

            @block.scalar
            def _(s, k=k, kbi=kbi):
                W(s, "ld%d" % kbi, deq[("ld", k)])
                if k >= 2:
                    W(s, "dv", deq[("am", k - 2)])
                z = scr[:, 3072 + kbi * 1024:4096 + kbi * 1024]
                u = scr[:, 5120 + kbi * 1024:6144 + kbi * 1024]
                I(s.activation(z, cod[:, kbi * 1024:(kbi + 1) * 1024], AF.Copy),
                  "ac")
                deq[("cast", k)] = ctr.val("ac")
                I(s.activation(u, z, AF.Square,
                               bias=actc_s[:, AC_M75:AC_M75 + 1]), "ac")
                deq[("sq", k)] = fence_ac(s)

            @block.vector
            def _(v, k=k, kbi=kbi):
                W(v, "ac", deq[("sq", k)])
                W(v, "la%d" % kbi, deq[("la", k)])
                if k >= 2:
                    W(v, "lgw%d" % kbi, deq[("st", k - 2)])
                z = scr[:, 3072 + kbi * 1024:4096 + kbi * 1024]
                wv = scr[:, 5120 + kbi * 1024:6144 + kbi * 1024]
                pev = scr[:, 7168:8192]
                pov = scr[:, 8192:9216]
                I(v.tensor_scalar(wv, wv, 1.0 / PS0, -PC0 / PS0, ALU.mult,
                                  ALU.add), "dv")
                I(v.tensor_scalar(z, z, -7.5, None, ALU.add), "dv")
                I(v.tensor_scalar(pev, wv, CE[7], None, ALU.mult), "dv")
                for j in range(6, 0, -1):
                    I(v.scalar_tensor_tensor(pev, pev, CE[j], wv, ALU.add,
                                             ALU.mult), "dv")
                I(v.tensor_scalar(pov, wv, CO[7], None, ALU.mult), "dv")
                for j in range(6, 0, -1):
                    I(v.scalar_tensor_tensor(pov, pov, CO[j], wv, ALU.add,
                                             ALU.mult), "dv")
                I(v.scalar_tensor_tensor(pov, pov, CO[0], z, ALU.add,
                                         ALU.mult), "dv")
                I(v.scalar_tensor_tensor(pev, pev, CE[0], pov, ALU.add,
                                         ALU.add), "dv")
                I(v.scalar_tensor_tensor(wout[:, kbi * 1024:(kbi + 1) * 1024],
                                         pev, 0.5,
                                         ambt[:, kbi * 1024:(kbi + 1) * 1024],
                                         ALU.mult, ALU.mult), "dv")
                deq[("am", k)] = fence_dv(v)

            if k + 2 < NT // 2:
                deq_load(k + 2)

            @block.gpsimd
            def _(g, k=k, kbi=kbi):
                W(g, "dv", deq[("am", k)])
                dst = AP(wsl_t, (k // 4) * 262144 + (k % 4) * 512,
                         [[2048, 128], [64, 8], [1, 64]])
                srcw = wout[:, kbi * 1024:(kbi + 1) * 1024].bitcast(F32)
                I(g.dma_start(out=dst, in_=srcw), "lgw%d" % kbi, 16)
                deq[("st", k)] = ctr.val("lgw%d" % kbi)

            if k % 4 == 3:
                q = k // 4

                @block.gpsimd
                def _(g, q=q, k=k):
                    W(g, "lgw0", deq[("st", k - 1)])
                    W(g, "lgw1", deq[("st", k)])
                    I(g.collective_compute(
                        "AllGather", ALU.bypass,
                        replica_groups=[list(range(NCORE))],
                        ins=[wslice.ap()[q * 262144:(q + 1) * 262144].opt()],
                        outs=[wfull.ap()[q * 2097152:(q + 1) * 2097152].opt()]), "cc")
                AGQ[q] = ctr.val("cc")

        lamre, lamim = sm8[:, 0:1], sm8[:, 1:2]
        cau = {}

        def emit_cauchy(ck):
            bi = ck % 2

            @block.sync
            def _(sync, ck=ck):
                if ck >= 1:
                    W(sync, "dv", cau[("dv", ck - 1)])
                I(sync.dma_start(out=outst[0:64, 0:512],
                                 in_=tbc_d[:, ck * 512:(ck + 1) * 512]), "lt", 16)
            cau[("ld", ck)] = ctr.val("lt")

            @block.vector
            def _(v, ck=ck, bi=bi):
                W(v, "lt", cau[("ld", ck)])
                if ck == 0:
                    W(v, "dv", DV_VT)
                if ck >= 1:
                    W(v, "pe", cau[("mm", ck - 1)])
                tb = outst[0:64, 0:512]
                dre = scr[0:64, 0:512]
                dim = scr[0:64, 512:1024]
                W(v, "dv")
                I(v.tensor_scalar(dre, tb, 0.0, lamre, ALU.mult, ALU.subtract), "dv")
                W(v, "dv")
                I(v.tensor_scalar(dim, tb, a2D[0:64, 0:1], lamim, ALU.mult,
                                  ALU.subtract), "dv")
                q1 = scr[0:64, 1024:1536]
                mag = scr[0:64, 1536:2048]
                W(v, "dv")
                I(v.tensor_mul(q1, dre, dre), "dv")
                W(v, "dv")
                I(v.tensor_mul(mag, dim, dim), "dv")
                W(v, "dv")
                I(v.tensor_add(mag, mag, q1), "dv")
                rmag = scr[0:64, 2048:2560]
                W(v, "dv")
                I(v.reciprocal(rmag, mag), "dv")
                iim = scr[0:64, 2560:3072]
                W(v, "dv")
                kv = han[:, 0:1024].bitcast(F32)
                W(v, "dv")
                I(v.tensor_mul(kv[0:64, :], dre, rmag), "dv")
                W(v, "dv")
                I(v.scalar_tensor_tensor(iim, dim, -1.0, rmag,
                                         ALU.mult, ALU.mult), "dv")
                W(v, "dv")
                I(v.tensor_copy(kv[64:128, :], iim), "dv")
                cau[("dv", ck)] = fence_dv(v)

            @block.tensor
            def _(t, ck=ck):
                W(t, "dv", cau[("dv", ck)])
                if ck >= 1:
                    W(t, "dv", cau.get(("ev", ck - 1), 0))
                I(t.matmul(pD[0:8, 0:512], vtile[:, :],
                           han[:, 0:1024].bitcast(F32),
                           start=True, stop=True), "pe")
                cau[("mm", ck)] = ctr.val("pe")

            @block.vector
            def _(v, ck=ck):
                W(v, "pe", cau[("mm", ck)])
                W(v, "dv")
                I(v.tensor_copy(outst[0:8, 512:1024], pD[0:8, 0:512]),
                  "dv")
                cau[("ev", ck)] = fence_dv(v)

            @block.gpsimd
            def _(g, ck=ck):
                W(g, "dv", cau[("ev", ck)])
                I(g.dma_start(out=AP(krows_t, ck * 512, [[4096, 8], [1, 512]]),
                              in_=outst[0:8, 512:1024]), "lkst", 16)
                cau[("kst", ck)] = ctr.val("lkst")
        for k in range(7):
            emit_pair(k)
        HOIST_XC0 = stats_and_xc(0, DV_VT, need_lg=LG_X)
        HOIST_TR0 = lora_traw(HOIST_XC0)
        HOIST_MS1 = stats_and_xc(1, HOIST_TR0, need_lg=LG_X, ms_only=True)
        for k in range(7, NT // 2):
            emit_pair(k)
            if k <= 14:
                emit_cauchy(k - 7)
            if k % 2 == 1:
                for s3 in range(2 * (k - 7), 2 * (k - 7) + 4):
                    ffn_slot(0, s3, ffn1_mile, HOIST_MS1)
        for s3 in range(20, NT):
            ffn_slot(0, s3, ffn1_mile, HOIST_MS1, wq="gpsimd")
        FFN1W0 = ffn_end(ffn1_mile)
        FFN1_DVEND = FFN1W0[0]
        CC_DONE = ctr.val("cc")

        LG_KROWS = ctr.val("lkst")
        CAU_SCR_DONE = cau[("dv", 7)]

        @block.sync
        def _(sync):
            W(sync, "lkst", LG_KROWS)
            for j in range(8):
                src = AP(krows_t, j * 4096, [[64, 64], [1, 64]])
                I(sync.dma_start(out=scr[0:64, j * 64:(j + 1) * 64], in_=src),
                  "lp", 16)
        LS_PLANES = ctr.val("lp")

        if "k" in dbg:
            @block.gpsimd
            def _(g):
                W(g, "lp", LS_PLANES)
                I(g.dma_start(out=dbg_t["dbg_cau"], in_=scr[0:64, 0:512]), "ldb", 16)
        LG_CAU = ctr.val("ldb")

        @block.vector
        def _(v):
            W(v, "lp", LS_PLANES)
            if "k" in dbg:
                W(v, "ldb", LG_CAU)
            s_ = lambda j: scr[0:64, j * 64:(j + 1) * 64]
            k00r, k00i, k01r, k01i, k10r, k10i, k11r, k11i = [s_(j) for j in range(8)]
            wr_ = s_(8)
            wi_ = s_(9)
            tq = s_(10)
            zr = s_(11)
            zi = k11i
            rzm = s_(12)
            W(v, "dv")
            I(v.tensor_mul(wr_, k01r, k10r), "dv")
            W(v, "dv")
            I(v.tensor_mul(tq, k01i, k10i), "dv")
            W(v, "dv")
            I(v.tensor_sub(wr_, wr_, tq), "dv")
            W(v, "dv")
            I(v.tensor_mul(wi_, k01r, k10i), "dv")
            W(v, "dv")
            I(v.tensor_mul(tq, k01i, k10r), "dv")
            W(v, "dv")
            I(v.tensor_add(wi_, wi_, tq), "dv")
            W(v, "dv")
            I(v.tensor_scalar(zr, k11r, 1.0, None, ALU.add), "dv")
            W(v, "dv")
            I(v.tensor_mul(tq, zr, zr), "dv")
            W(v, "dv")
            I(v.tensor_mul(rzm, zi, zi), "dv")
            W(v, "dv")
            I(v.tensor_add(tq, tq, rzm), "dv")
            W(v, "dv")
            I(v.reciprocal(rzm, tq), "dv")
            cr, ci = k01r, k01i
            W(v, "dv")
            I(v.tensor_mul(tq, wr_, zr), "dv")
            W(v, "dv")
            I(v.tensor_mul(cr, wi_, zi), "dv")
            W(v, "dv")
            I(v.tensor_add(tq, tq, cr), "dv")
            W(v, "dv")
            I(v.tensor_mul(cr, tq, rzm), "dv")
            W(v, "dv")
            I(v.tensor_mul(tq, wi_, zr), "dv")
            W(v, "dv")
            I(v.tensor_mul(ci, wr_, zi), "dv")
            W(v, "dv")
            I(v.tensor_sub(tq, tq, ci), "dv")
            W(v, "dv")
            I(v.tensor_mul(ci, tq, rzm), "dv")
            gr, gi = k00r, k00i
            W(v, "dv")
            I(v.tensor_sub(gr, k00r, cr), "dv")
            W(v, "dv")
            I(v.tensor_sub(gi, k00i, ci), "dv")
            W(v, "dv")
            I(v.tensor_mul(tq, t64_s[:, :], gi), "dv")
            W(v, "dv")
            I(v.tensor_sub(arre[0:64, :], gr, tq), "dv")
            W(v, "dv")
            I(v.tensor_mul(tq, t64_s[:, :], gr), "dv")
            W(v, "dv")
            I(v.tensor_add(arim[0:64, :], gi, tq), "dv")
            fence_dv(v)
        DV_AR = ctr.val("dv")

        @block.tensor
        def _(t):
            W(t, "dv", DV_AR)
            I(t.matmul(pStat[0:64, 0:64], e64c_s[:, :], arre[:, :],
                       start=True, stop=False), "pe")
            I(t.matmul(pStat[0:64, 0:64], e64sn_s[:, :], arim[:, :],
                       start=False, stop=True), "pe")
            I(t.matmul(pStat[0:64, 512:576], e64c_s[:, :], arim[:, :],
                       start=True, stop=False), "pe")
            I(t.matmul(pStat[0:64, 512:576], e64s_s[:, :], arre[:, :],
                       start=False, stop=True), "pe")
        PE_S = ctr.val("pe")

        @block.vector
        def _(v):
            W(v, "pe", PE_S)
            sre = pStat[0:64, 0:64]
            sim = pStat[0:64, 512:576]
            tq1 = idt[0:64, 128:192]
            W(v, "dv")
            I(v.tensor_mul(idt[0:64, 0:64], sre, twc_s[:, :]), "dv")
            W(v, "dv")
            I(v.tensor_mul(tq1, sim, tws_s[:, :]), "dv")
            W(v, "dv")
            I(v.tensor_sub(idt[0:64, 0:64], idt[0:64, 0:64], tq1), "dv")
            W(v, "dv")
            I(v.tensor_mul(idt[0:64, 64:128], sre, tws_s[:, :]), "dv")
            W(v, "dv")
            I(v.tensor_mul(tq1, sim, twc_s[:, :]), "dv")
            W(v, "dv")
            I(v.tensor_add(idt[0:64, 64:128], idt[0:64, 64:128], tq1), "dv")
            fence_dv(v)
        DV_T2 = ctr.val("dv")

        @block.tensor
        def _(t):
            W(t, "dv", DV_T2)
            I(t.transpose(pStat[0:64, 0:128], idt[:, 0:64], id32[:, :]), "pe")
            I(t.transpose(pStat[0:64, 512:640], idt[:, 64:128], id32[:, :]), "pe")
        PE_TT = ctr.val("pe")

        @block.vector
        def _(v):
            W(v, "pe", PE_TT)
            W(v, "dv")
            I(v.tensor_copy(idtT[0:64, 0:64], pStat[0:64, 0:64]), "dv")
            W(v, "dv")
            I(v.tensor_copy(idtT[0:64, 64:128], pStat[0:64, 512:576]), "dv")
            fence_dv(v)
        DV_T2T = ctr.val("dv")

        @block.tensor
        def _(t):
            W(t, "dv", DV_T2T)
            I(t.matmul(pD[0:64, 0:64], e64cL_s[:, :], idtT[:, 0:64],
                       start=True, stop=False), "pe")
            I(t.matmul(pD[0:64, 0:64], e64snL_s[:, :], idtT[:, 64:128],
                       start=False, stop=True), "pe")
        PE_K = ctr.val("pe")

        @block.vector
        def _(v):
            W(v, "pe", PE_K)
            W(v, "dv")
            I(v.tensor_copy(k16t[:, :], pD[0:64, 0:64]), "dv")
            fence_dv(v)
        DV_K16 = ctr.val("dv")

        @block.gpsimd
        def _(g):
            W(g, "dv", DV_K16)
            W(g, "lkst", LG_KROWS)
            I(g.dma_start(out=AP(kpad_t, 0, [[1, 1], [1, 511]]),
                          in_=zrow[0:1, 0:511]), "lk", 16)
            I(g.dma_start(out=AP(kpad_t, 4607, [[1, 1], [1, 257]]),
                          in_=zrow[0:1, 0:257]), "lk", 16)
            I(g.dma_start(out=AP(kpad_t, 511, [[64, 64], [1, 64]]),
                          in_=k16t[:, :]), "lk", 16)
        LG_KPAD = ctr.val("lk")

        def emit_tbuild():
         tb_m = {}
         for d in range(8):
            bi = d % 2

            @block.sync
            def _(sync, d=d, bi=bi):
                if d == 0:
                    W(sync, "lk", LG_KPAD)
                if d >= 2:
                    W(sync, "pe", tb_m[("mm", d - 2)])
                src = AP(kpad_t, 512 * d, [[1, 128], [128, 7], [1, 128]])
                I(sync.dma_start(out=han[:, bi * 896:(bi + 1) * 896], in_=src),
                  "lh%d" % bi, 16)
            tb_m[("ld", d)] = ctr.val("lh%d" % bi)

            @block.tensor
            def _(t, d=d, bi=bi):
                W(t, "lh%d" % bi, tb_m[("ld", d)])
                if d >= 1:
                    W(t, "ac", tb_m[("ev", d - 1)])
                p1 = pB[:, 0:512] if bi == 0 else pE[:, 0:512]
                p2 = pD[:, 0:512] if bi == 0 else pB[:, 0:512]
                I(t.matmul(p1[:, 0:512], jm16[:, :],
                           han[:, bi * 896:bi * 896 + 512],
                           start=True, stop=True), "pe")
                I(t.matmul(p2[:, 0:384], jm16[:, :],
                           han[:, bi * 896 + 512:(bi + 1) * 896],
                           start=True, stop=True), "pe")
                tb_m[("mm", d)] = ctr.val("pe")

            @block.scalar
            def _(s, d=d, bi=bi):
                W(s, "pe", tb_m[("mm", d)])
                p1 = pB[:, 0:512] if bi == 0 else pE[:, 0:512]
                p2 = pD[:, 0:512] if bi == 0 else pB[:, 0:512]
                I(s.activation(t16[:, d * 896:d * 896 + 512], p1[:, 0:512],
                               AF.Copy), "ac")
                I(s.activation(t16[:, d * 896 + 512:(d + 1) * 896], p2[:, 0:384],
                               AF.Copy), "ac")
                tb_m[("ev", d)] = fence_ac(s)
         return ctr.val("ac")

        AC_T16 = emit_tbuild()

        prev_dv = None
        prev_cft = None
        last_mm = 0
        last_st = None
        for w in range(NW):
            if w == 0:
                dv_f1, last_mm, _, cft1 = FFN1W0
            else:
                @block.vector
                def _(v):
                    W(v, "dv", prev_dv)
                    W(v, "dv", HOIST_MS1)
                    I(v.tensor_copy(sbc[:, :], sbc1[:, :]), "dv")
                    for i in range(NT):
                        I(v.scalar_tensor_tensor(xct(i), xt(i, w), INV128,
                                                 mbc1[:, :],
                                                 ALU.mult, ALU.subtract), "dv")
                    fence_dv(v)
                dv_xc = ctr.val("dv")
                dv_tr = lora_traw(dv_xc)
                dv_f1, last_mm, _, cft1 = ffn(w, dv_tr, final=False,
                                              prev_mm=last_mm)
            if stop_after == "s1":
                prev_dv = dv_f1
                prev_cft = cft1
                continue
            dv_xc2 = stats_and_xc(w, dv_f1, prev_c=cft1)
            dv_c, cftc = conv2(w, dv_xc2)
            if stop_after == "s2":
                prev_dv = dv_c
                prev_cft = cftc
                continue
            dv_xc3 = stats_and_xc(w, dv_c, prev_c=cftc)
            dv_tr3 = lora_traw(dv_xc3)
            prev_dv, last_mm, last_st, prev_cft = ffn(w, dv_tr3, final=True,
                                                      prev_mm=last_mm,
                                                      prev_st=last_st)
        @block.gpsimd
        def _(g):
            if "k" in dbg:
                W(g, "ac", AC_T16)
                I(g.dma_start(out=scr[0:1, 0:2432].bitcast(F16),
                              in_=AP(kpad_t, 0, [[1, 1], [1, 4864]])), "ldb", 16)
                W(g, "ldb", ctr.val("ldb"))
                I(g.dma_start(out=AP(dbg_t["dbg_k"].tensor, 0, [[1, 1], [1, 4864]]),
                              in_=scr[0:1, 0:2432].bitcast(F16)), "ldb", 16)
                I(g.dma_start(out=dbg_t["dbg_t0"], in_=t16[:, 3 * 128:4 * 128]), "ldb", 16)
                I(g.dma_start(out=dbg_t["dbg_vt"], in_=vtile[:, :]), "ldb", 16)
                I(g.dma_start(out=dbg_t["dbg_kc"], in_=kcols[:, :]), "ldb", 16)
                I(g.dma_start(out=dbg_t["dbg_a2",], in_=a2D[:, :]), "ldb", 16)
                I(g.dma_start(out=dbg_t["dbg_kinv"], in_=han[:, 0:1024].bitcast(F32)), "ldb", 16)
                I(g.dma_start(out=dbg_t["dbg_ar"][:, 0:64], in_=arre[:, :]), "ldb", 16)
                I(g.dma_start(out=dbg_t["dbg_ar"][:, 64:128], in_=arim[:, :]), "ldb", 16)
            if "x" in dbg:
                W(g, "dv", prev_dv)
                for i in range(NT):
                    I(g.dma_start(out=dbg_t["dbg_x"][i * 128:(i + 1) * 128, :],
                                  in_=xt(i)), "ldb", 16)

        @block.gpsimd
        def _(g):
            for n in ["lc", "lx", "ld0", "ld1", "lgw0", "lgw1", "lt", "lp",
                      "lh0", "lh1", "lw0", "lw1", "lw2", "lk", "lkst",
                      "lo0", "lo1", "ldb"]:
                W(g, n, ctr.val(n))

    return nc


def kernel(**inputs):
    nc = build_program()
    in_maps = stage_inputs(inputs)
    res = run_bass_kernel_spmd(nc, in_maps, core_ids=list(range(NCORE)))
    parts = [np.asarray(r["outT"], dtype=np.float32) for r in res.results]
    full = np.concatenate(parts, axis=1)
    return np.ascontiguousarray(full.T).reshape(2, 4096, 4096).astype(np.float32)


# version 14
# speedup vs baseline: 1.0504x; 1.0504x over previous
import numpy as np
import ml_dtypes
import concourse.bass as bass
import concourse.mybir as mybir
from concourse.bass import AP
from concourse.bass_utils import run_bass_kernel_spmd

F32 = mybir.dt.float32
F16 = mybir.dt.float16
BF16 = mybir.dt.bfloat16
U8 = mybir.dt.uint8
AF = mybir.ActivationFunctionType
ALU = mybir.AluOpType
X = mybir.AxisListType.X

NF4 = np.array([-1.0, -0.6961928009986877, -0.5250730514526367, -0.39491748809814453,
                -0.28444138169288635, -0.18477343022823334, -0.09105003625154495, 0.0,
                0.07958029955625534, 0.16093020141124725, 0.24611230194568634,
                0.33791524171829224, 0.44070982933044434, 0.5626170039176941,
                0.7229568362236023, 1.0], dtype=np.float64)

NCORE = 8
RPC = 1024
WAVE = 512
NW = RPC // WAVE
NT = 32
OSL = 512
PC0, PS0 = 28.25, 28.0
INV128 = 1.0 / 128.0
AC_M75, AC_LN2, AC_NEG1, AC_NEGH, AC_EPS, AC_I4096, AC_HALF, AC_128 = range(8)


def nf4_coeffs():
    k = np.arange(16, dtype=np.float64)
    z = k - 7.5
    u = z * z
    uv = np.unique(u)
    even = 0.5 * (NF4 + NF4[::-1])
    odd = 0.5 * (NF4 - NF4[::-1]) / z
    idx = [np.where(u == v)[0][0] for v in uv]
    w = (uv - PC0) / PS0
    V = np.vander(w, 8, increasing=True)
    return ([float(v) for v in np.linalg.solve(V, even[idx])],
            [float(v) for v in np.linalg.solve(V, odd[idx])])


CE, CO = nf4_coeffs()

TSLOTS = []
TSLOT_OF = {}
for d in range(8):
    for a in range(4):
        for b in range(4):
            if d == 0 and b > a:
                continue
            TSLOT_OF[(d, a, b)] = len(TSLOTS)
            TSLOTS.append((d, a, b))
assert len(TSLOTS) == 122


def host_constants():
    f = np.float32
    c = {}
    jm = np.zeros((128, 128), np.float16)
    jm[np.arange(128), 127 - np.arange(128)] = 1.0
    c["jmat"] = jm
    c["ones16"] = np.ones((128, 128), np.float16)
    c["onescol16"] = np.ones((128, 1), np.float16)
    c["onescolbf"] = np.ones((128, 1), ml_dtypes.bfloat16)
    c["ident32"] = np.eye(128, dtype=f)
    ang64 = 2 * np.pi * np.outer(np.arange(64), np.arange(64)) / 64
    e64c = np.cos(ang64)
    e64s = np.sin(ang64)

    def pad(m, scale=1.0):
        out = np.zeros((128, 64), f)
        out[:64, :] = (m * scale).astype(f)
        return out

    c["e64c_p"] = pad(e64c)
    c["e64s_p"] = pad(e64s)
    c["e64sn_p"] = pad(-e64s)
    c["e64cL_p"] = pad(e64c, 1.0 / 4096.0)
    c["e64snL_p"] = pad(-e64s, 1.0 / 4096.0)
    angT = 2 * np.pi * np.outer(np.arange(64), np.arange(64)) / 4096
    c["twc"] = np.cos(angT).astype(f)
    c["tws"] = np.sin(angT).astype(f)
    t = np.tan(np.pi * np.arange(4096) / 4096)
    c["t64"] = t.reshape(64, 64).astype(f)
    c["tbc"] = np.tile(t.astype(f)[None, :], (64, 1))
    ac = np.zeros((128, 8), f)
    ac[:, AC_M75] = -7.5
    ac[:, AC_LN2] = np.log(2.0)
    ac[:, AC_NEG1] = -1.0
    ac[:, AC_NEGH] = -0.5
    ac[:, AC_EPS] = 1e-5
    ac[:, AC_I4096] = 1.0 / 4096.0
    ac[:, AC_HALF] = 0.5
    ac[:, AC_128] = 128.0
    c["actc"] = ac
    return c


def stage_inputs(inputs):
    consts = host_constants()
    x = np.ascontiguousarray(inputs["L0_attn_u"].reshape(8192, 4096).T)
    codes_u8 = np.ascontiguousarray(inputs["L0_ffn_w_codes"].view(np.uint8)[:, 0::4])
    codesT = np.ascontiguousarray(codes_u8.T)
    amB = np.repeat(np.ascontiguousarray(inputs["L0_ffn_absmax"].T), 64, axis=0)

    def omaj(cs):
        t = cs.reshape(4, 8, 128, 4, 128).transpose(3, 0, 2, 1, 4)
        return np.ascontiguousarray(t.reshape(2048, 1024))
    small8 = np.stack([
        inputs["L0_attn_Lambda_ri"][:, 0], inputs["L0_attn_Lambda_ri"][:, 1],
        inputs["L0_attn_p_ri"][:, 0], inputs["L0_attn_p_ri"][:, 1],
        inputs["L0_attn_q_ri"][:, 0], inputs["L0_attn_q_ri"][:, 1],
        inputs["L0_attn_Ct_ri"][0, :, 0], inputs["L0_attn_Ct_ri"][0, :, 1],
    ], axis=1).astype(np.float32)
    in_maps = []
    for cid in range(NCORE):
        m = dict(consts)
        m["xT"] = np.ascontiguousarray(x[:, cid * RPC:(cid + 1) * RPC])
        m["codesT"] = omaj(codesT[:, cid * OSL:(cid + 1) * OSL])
        m["amB"] = omaj(amB[:, cid * OSL:(cid + 1) * OSL].astype(np.float32))
        m["waf"] = np.ascontiguousarray(inputs["L0_ffn_W_a"].astype(np.float32))
        m["wbf"] = np.ascontiguousarray(inputs["L0_ffn_W_b"].astype(np.float32))
        m["small8"] = small8
        m["vcre"] = np.ascontiguousarray(inputs["L0_attn_Vc_ri"][:, :, 0]).astype(np.float32)
        m["vcim"] = np.ascontiguousarray(inputs["L0_attn_Vc_ri"][:, :, 1]).astype(np.float32)
        m["bpb"] = np.tile(inputs["L0_attn_Bp"].reshape(1, 64), (64, 1)).astype(np.float32)
        m["dls"] = np.array([[float(inputs["L0_attn_D"][0]),
                              float(inputs["L0_attn_log_step"][0])]], dtype=np.float32)
        in_maps.append(m)
    return in_maps


class Ctr:
    def __init__(self):
        self.c = {}

    def inc(self, name, n=1):
        self.c[name] = self.c.get(name, 0) + n
        return self.c[name]

    def val(self, name):
        return self.c.get(name, 0)


def build_program(stop_after=None, dbg=()):
    nc = bass.Bass("TRN2", target_bir_lowering=False, debug=False)
    ctr = Ctr()

    din = lambda n, s, d: nc.dram_tensor(n, s, d, kind="ExternalInput").ap()
    xT = din("xT", [4096, RPC], F32)
    codesT = din("codesT", [2048, 1024], U8)
    amB = din("amB", [2048, 1024], F32)
    waf = din("waf", [4096, 16], F32)
    wbf = din("wbf", [16, 4096], F32)
    small8 = din("small8", [64, 8], F32)
    vcre = din("vcre", [64, 64], F32)
    vcim = din("vcim", [64, 64], F32)
    bpb = din("bpb", [64, 64], F32)
    dls = din("dls", [1, 2], F32)
    jmat = din("jmat", [128, 128], F16)
    ones16d = din("ones16", [128, 128], F16)
    onescol16 = din("onescol16", [128, 1], F16)
    onescolbf = din("onescolbf", [128, 1], BF16)
    ident32d = din("ident32", [128, 128], F32)
    e64c_p = din("e64c_p", [128, 64], F32)
    e64s_p = din("e64s_p", [128, 64], F32)
    e64sn_p = din("e64sn_p", [128, 64], F32)
    e64cL_p = din("e64cL_p", [128, 64], F32)
    e64snL_p = din("e64snL_p", [128, 64], F32)
    twc_d = din("twc", [64, 64], F32)
    tws_d = din("tws", [64, 64], F32)
    t64_d = din("t64", [64, 64], F32)
    tbc_d = din("tbc", [64, 4096], F32)
    actc_d = din("actc", [128, 8], F32)

    outT = nc.dram_tensor("outT", [4096, RPC], F32, kind="ExternalOutput").ap()

    wslice = nc.dram_tensor("wslice", [4 * NT * 128 * 64], F32, kind="Internal")
    wfull = nc.dram_tensor("wfull", [32 * NT * 128 * 64], F32, kind="Internal",
                           addr_space="Shared")
    kpad = nc.dram_tensor("kpad", [4864], F16, kind="Internal")
    krowsd = nc.dram_tensor("krowsd", [8 * 4096], F32, kind="Internal")

    dbg_t = {}
    if "k" in dbg:
        dbg_t["dbg_k"] = nc.dram_tensor("dbg_k", [4864], F16, kind="ExternalOutput").ap()
        dbg_t["dbg_t0"] = nc.dram_tensor("dbg_t0", [128, 128], F16, kind="ExternalOutput").ap()
        dbg_t["dbg_ar"] = nc.dram_tensor("dbg_ar", [128, 128], F32, kind="ExternalOutput").ap()
        dbg_t["dbg_cau"] = nc.dram_tensor("dbg_cau", [64, 512], F32, kind="ExternalOutput").ap()
        dbg_t["dbg_vt"] = nc.dram_tensor("dbg_vt", [128, 8], F32, kind="ExternalOutput").ap()
        dbg_t["dbg_kc"] = nc.dram_tensor("dbg_kc", [64, 16], F32, kind="ExternalOutput").ap()
        dbg_t["dbg_a2", ] = nc.dram_tensor("dbg_a2", [128, 4], F32, kind="ExternalOutput").ap()
        dbg_t["dbg_kinv"] = nc.dram_tensor("dbg_kinv", [128, 512], F32, kind="ExternalOutput").ap()
    if "x" in dbg:
        dbg_t["dbg_x"] = nc.dram_tensor("dbg_x", [4096, RPC], F16, kind="ExternalOutput").ap()
    if "w" in dbg:
        dbg_t["dbg_w"] = nc.dram_tensor("dbg_w", [4 * NT * 128 * 64], F32,
                                        kind="ExternalOutput").ap()

    from contextlib import ExitStack
    es = ExitStack()
    sbt = lambda name, shape, dtype: es.enter_context(nc.sbuf_tensor(name, shape, dtype))
    pst = lambda name, shape: es.enter_context(nc.psum_tensor(name, shape, F32))

    x16 = sbt("x16", [128, NT * RPC], F16)
    xc16 = sbt("xc16", [128, NT * WAVE], F16)
    t16 = sbt("t16", [128, 56 * 128], F16)
    wb16 = sbt("wb16", [128, 4096], F16)
    wa16 = sbt("wa16", [128, NT * 16], F16)
    wbuf = sbt("wbuf", [128, 3 * 2048], F16)
    traw16 = sbt("traw16", [128, WAVE], F16)
    mbc = sbt("mbc", [128, WAVE], F16)
    sbc = sbt("sbc", [128, WAVE], F16)
    mbc1 = sbt("mbc1", [128, WAVE], F16)
    sbc1 = sbt("sbc1", [128, WAVE], F16)
    rp = sbt("rp", [128, 2 * WAVE], F16)
    cod = sbt("cod", [128, 2 * 1024], U8)
    ambt = sbt("ambt", [128, 2 * 1024], F16)
    scr = sbt("scr", [128, 18 * 512], F32)
    wout = sbt("wout", [128, 2 * 1024], F16)
    xsq = sbt("xsq", [128, 2 * 512], BF16)
    raws = sbt("raws", [128, 2 * 512], F16)
    cbuf = sbt("cbuf", [128, 2 * 512], F16)
    outst = sbt("outst", [128, 2 * 512], F32)
    jm16 = sbt("jm16", [128, 128], F16)
    idd16 = sbt("idd16", [128, 128], F16)
    on16 = sbt("on16", [128, 128], F16)
    onc16 = sbt("onc16", [128, 1], F16)
    oncbf = sbt("oncbf", [128, 1], BF16)
    id32 = sbt("id32", [128, 128], F32)
    e64c_s = sbt("e64c_s", [128, 64], F32)
    e64s_s = sbt("e64s_s", [128, 64], F32)
    e64sn_s = sbt("e64sn_s", [128, 64], F32)
    e64cL_s = sbt("e64cL_s", [128, 64], F32)
    e64snL_s = sbt("e64snL_s", [128, 64], F32)
    twc_s = sbt("twc_s", [64, 64], F32)
    tws_s = sbt("tws_s", [64, 64], F32)
    t64_s = sbt("t64_s", [64, 64], F32)
    actc_s = sbt("actc_s", [128, 8], F32)
    sm8 = sbt("sm8", [64, 8], F32)
    vcre_s = sbt("vcre_s", [64, 64], F32)
    vcim_s = sbt("vcim_s", [64, 64], F32)
    bpb_s = sbt("bpb_s", [64, 64], F32)
    dls_s = sbt("dls_s", [1, 2], F32)
    kcols = sbt("kcols", [64, 16], F32)
    vtile = sbt("vtile", [128, 8], F32)
    acol = sbt("acol", [128, 4], F32)
    a2D = sbt("a2D", [128, 4], F32)
    ones32 = sbt("ones32", [128, 128], F32)
    rowsc = sbt("rowsc", [1, 16], F32)
    arre = sbt("arre", [128, 64], F32)
    arim = sbt("arim", [128, 64], F32)
    idt = sbt("idt", [128, 3 * 64], F32)
    idtT = sbt("idtT", [128, 2 * 64], F32)
    k16t = sbt("k16t", [64, 64], F16)
    zrow = sbt("zrow", [1, 511], F16)
    han = sbt("han", [128, 2 * 896], F16)

    pA = pst("pA", [128, 512])
    pB = pst("pB", [128, 512])
    pC = pst("pC", [128, 512])
    pStat = pst("pStat", [128, 1024])
    pD = pst("pD", [128, 512])
    pE = pst("pE", [128, 512])

    sems = {n: es.enter_context(nc.semaphore(n))
            for n in ["lc", "lx", "ld0", "ld1", "lgw0", "lgw1", "lt", "lp",
                      "lh0", "lh1", "lw0", "lw1", "lw2", "lk", "lkst",
                      "lo0", "lo1", "ldb", "la0", "la1", "pe", "dv", "ac", "cc"]}
    block = es.enter_context(nc.Block())

    def W(eng, name, val=None):
        eng.wait_ge(sems[name], ctr.val(name) if val is None else val)

    def I(instr, name, n=1):
        instr.then_inc(sems[name], n)
        return ctr.inc(name, n)

    def fence_dv(v):
        W(v, "dv")
        return I(v.tensor_copy(rowsc[0:1, 8:9], rowsc[0:1, 9:10]), "dv")

    def fence_ac(s):
        W(s, "ac")
        return I(s.activation(rowsc[0:1, 10:11], rowsc[0:1, 11:12], AF.Copy), "ac")

    def xt(i, w=None):
        if w is None:
            return x16[:, i * RPC:(i + 1) * RPC]
        return x16[:, i * RPC + w * WAVE: i * RPC + (w + 1) * WAVE]

    def xct(i):
        return xc16[:, i * WAVE:(i + 1) * WAVE]

    wsl_t = wslice.ap().tensor
    wfull_t = wfull.ap().tensor
    kpad_t = kpad.ap().tensor
    krows_t = krowsd.ap().tensor

    with es:
        @block.sync
        def _(sync):
            for dst, src in [(jm16, jmat), (on16, ones16d), (onc16, onescol16),
                             (oncbf, onescolbf), (id32, ident32d),
                             (e64c_s, e64c_p), (e64s_s, e64s_p), (e64sn_s, e64sn_p),
                             (e64cL_s, e64cL_p), (e64snL_s, e64snL_p),
                             (twc_s, twc_d), (tws_s, tws_d), (t64_s, t64_d),
                             (actc_s, actc_d), (sm8, small8), (vcre_s, vcre),
                             (vcim_s, vcim), (bpb_s, bpb), (dls_s, dls)]:
                I(sync.dma_start(out=dst[:, :], in_=src), "lc", 16)
        LS_CONST = ctr.val("lc")

        deq = {}
        AGQ = {}
        ffn1_mile = {}

        def deq_load(k):
            kbi = k % 2

            @block.gpsimd
            def _(g, k=k, kbi=kbi):
                if k >= 2:
                    W(g, "ac", deq[("cast", k - 2)])
                    W(g, "dv", deq[("am", k - 2)])
                srcc = AP(codesT.tensor, k * 131072, [[1024, 128], [1, 1024]])
                I(g.dma_start(out=cod[:, kbi * 1024:(kbi + 1) * 1024], in_=srcc),
                  "ld%d" % kbi, 16)
                srca = AP(amB.tensor, k * 131072, [[1024, 128], [1, 1024]])
                I(g.dma_start(out=ambt[:, kbi * 1024:(kbi + 1) * 1024], in_=srca),
                  "la%d" % kbi, 16)
            deq[("ld", k)] = ctr.val("ld%d" % kbi)
            deq[("la", k)] = ctr.val("la%d" % kbi)
        deq_load(0)
        deq_load(1)

        @block.gpsimd
        def _(g):
            for i in range(NT):
                I(g.dma_start(out=xt(i), in_=xT[i * 128:(i + 1) * 128, :]), "lx", 16)
            for i in range(NT):
                I(g.dma_start(out=wa16[:, i * 16:(i + 1) * 16],
                              in_=waf[i * 128:(i + 1) * 128, :]), "lx", 16)
        LG_X = ctr.val("lx")

        @block.vector
        def _(v):
            W(v, "lc", LS_CONST)
            for buf in [rp, arre, arim, idtT, traw16, wb16, zrow, acol, idt,
                        rowsc]:
                W(v, "dv")
                I(v.memset(buf[:, :], 0.0), "dv")
            W(v, "dv")
            I(v.memset(ones32[:, :], 1.0), "dv")
            fence_dv(v)
        DV_MEM = ctr.val("dv")

        @block.gpsimd
        def _(g):
            W(g, "dv", DV_MEM)
            I(g.dma_start(out=wb16[0:16, :], in_=wbf), "lx", 16)
        LG_WB = ctr.val("lx")

        @block.scalar
        def _(s):
            import concourse.bass_isa as bass_isa
            s.add_instruction(mybir.InstLoadActFuncSet(
                name=nc.get_next_instruction_name() if hasattr(nc, "get_next_instruction_name") else "I-preload",
                act_func_set_id=6, ins=[], outs=[]))
            W(s, "lc", LS_CONST)
            W(s, "dv", DV_MEM)
            W(s, "ac")
            I(s.activation(rowsc[0:1, 0:1], dls_s[0:1, 1:2], AF.Exp,
                           bias=actc_s[0:1, AC_LN2:AC_LN2 + 1],
                           scale=actc_s[0:1, AC_NEG1:AC_NEG1 + 1]), "ac")
            fence_ac(s)
        AC_A2 = ctr.val("ac")

        @block.vector
        def _(v):
            W(v, "dv", DV_MEM)
            W(v, "ac", AC_A2)
            W(v, "dv")
            I(v.tensor_copy(acol[0:1, 0:1], rowsc[0:1, 0:1]), "dv")
            W(v, "dv")
            I(v.tensor_copy(acol[0:1, 1:2], dls_s[0:1, 0:1]), "dv")
            fence_dv(v)
        DV_ACOL = ctr.val("dv")

        @block.tensor
        def _(t):
            W(t, "dv", DV_ACOL)
            I(t.matmul(pC[:, 0:4], ones32[:, :], acol[:, 0:4],
                       start=True, stop=True), "pe")
        PE_A2 = ctr.val("pe")

        @block.vector
        def _(v):
            W(v, "pe", PE_A2)
            W(v, "dv")
            I(v.tensor_copy(a2D[:, :], pC[:, 0:4]), "dv")
            W(v, "dv")
            I(v.tensor_mul(scr[0:64, 0:64], vcre_s[:, :], bpb_s[:, :]), "dv")
            W(v, "dv")
            I(v.reduce_sum(kcols[:, 0:1], scr[0:64, 0:64], axis=X), "dv")
            W(v, "dv")
            I(v.tensor_mul(scr[0:64, 0:64], vcim_s[:, :], bpb_s[:, :]), "dv")
            W(v, "dv")
            I(v.reduce_sum(kcols[:, 1:2], scr[0:64, 0:64], axis=X), "dv")
            ctre, ctim = sm8[:, 6:7], sm8[:, 7:8]
            pre_, pim_ = sm8[:, 2:3], sm8[:, 3:4]
            qre_, qim_ = sm8[:, 4:5], sm8[:, 5:6]
            bre, bim = kcols[:, 0:1], kcols[:, 1:2]
            nctim = kcols[:, 14:15]
            nqim = kcols[:, 13:14]
            W(v, "dv")
            I(v.tensor_scalar(nctim, ctim, -1.0, None, ALU.mult), "dv")
            W(v, "dv")
            I(v.tensor_scalar(nqim, qim_, -1.0, None, ALU.mult), "dv")
            tmpc = kcols[:, 15:16]

            def cmulcols(ar_, ai_, br_, bi_, o_re, o_im):
                W(v, "dv")
                I(v.tensor_mul(o_re, ar_, br_), "dv")
                W(v, "dv")
                I(v.tensor_mul(tmpc, ai_, bi_), "dv")
                W(v, "dv")
                I(v.tensor_sub(o_re, o_re, tmpc), "dv")
                W(v, "dv")
                I(v.tensor_mul(o_im, ar_, bi_), "dv")
                W(v, "dv")
                I(v.tensor_mul(tmpc, ai_, br_), "dv")
                W(v, "dv")
                I(v.tensor_add(o_im, o_im, tmpc), "dv")

            cmulcols(ctre, nctim, bre, bim, kcols[:, 2:3], kcols[:, 3:4])
            cmulcols(ctre, nctim, pre_, pim_, kcols[:, 4:5], kcols[:, 5:6])
            cmulcols(qre_, nqim, bre, bim, kcols[:, 6:7], kcols[:, 7:8])
            cmulcols(qre_, nqim, pre_, pim_, kcols[:, 8:9], kcols[:, 9:10])
            for j, (rc, ic) in enumerate([(2, 3), (4, 5), (6, 7), (8, 9)]):
                vre = kcols[:, rc:rc + 1]
                vim = kcols[:, ic:ic + 1]
                W(v, "dv")
                I(v.tensor_copy(vtile[0:64, 2 * j:2 * j + 1], vre), "dv")
                W(v, "dv")
                I(v.tensor_scalar(vtile[64:128, 2 * j:2 * j + 1], vim, -1.0,
                                  None, ALU.mult), "dv")
                W(v, "dv")
                I(v.tensor_copy(vtile[0:64, 2 * j + 1:2 * j + 2], vim), "dv")
                W(v, "dv")
                I(v.tensor_copy(vtile[64:128, 2 * j + 1:2 * j + 2], vre), "dv")
            fence_dv(v)
        DV_VT = ctr.val("dv")

        @block.scalar
        def _(s):
            W(s, "dv", DV_VT)
            W(s, "ac")
            I(s.activation(idd16[:, :], id32[:, :], AF.Copy,
                           scale=a2D[:, 1:2]), "ac")
            fence_ac(s)
        AC_IDD = ctr.val("ac")

        def stats_and_xc(w, prev_dv, need_lg=None, prev_c=None, ms_only=False):
            st = {}
            mb_out = mbc1 if ms_only else mbc
            sb_out = sbc1 if ms_only else sbc

            @block.tensor
            def _(t):
                if prev_c is None:
                    W(t, "dv", prev_dv)
                if need_lg is not None:
                    W(t, "lx", need_lg)
                for i in range(NT):
                    if prev_c is not None:
                        W(t, "dv", prev_c[i])
                    I(t.matmul(pStat[0:1, 0:512], onc16[:, :], xt(i, w),
                               start=(i == 0), stop=(i == NT - 1)), "pe")

            for i in range(NT):
                bi = i % 2

                @block.scalar
                def _(s, i=i, bi=bi):
                    if i == 0:
                        if prev_c is None:
                            W(s, "dv", prev_dv)
                        if need_lg is not None:
                            W(s, "lx", need_lg)
                    if prev_c is not None:
                        W(s, "dv", prev_c[i])
                    if i >= 2:
                        W(s, "pe", st[("mm", i - 2)])
                    I(s.activation(xsq[:, bi * 512:(bi + 1) * 512], xt(i, w),
                                   AF.Square), "ac")
                    st[("sq", i)] = fence_ac(s)

                @block.tensor
                def _(t, i=i, bi=bi):
                    W(t, "ac", st[("sq", i)])
                    I(t.matmul(pStat[0:1, 512:1024], oncbf[:, :],
                               xsq[:, bi * 512:(bi + 1) * 512],
                               start=(i == 0), stop=(i == NT - 1)), "pe")
                    st[("mm", i)] = ctr.val("pe")
            PE_ST = ctr.val("pe")

            mrow = cbuf[0:1, 0:1024].bitcast(F32)
            qrow = xsq[0:1, 0:1024].bitcast(F32)
            mm2 = raws[0:1, 0:1024].bitcast(F32)

            @block.vector
            def _(v):
                W(v, "pe", PE_ST)
                W(v, "dv")
                I(v.tensor_scalar(mrow, pStat[0:1, 0:512], 1.0 / 4096.0, None,
                                  ALU.mult), "dv")
                W(v, "dv")
                I(v.tensor_scalar(qrow, pStat[0:1, 512:1024], 1.0 / 4096.0, None,
                                  ALU.mult), "dv")
                W(v, "dv")
                I(v.tensor_mul(mm2, mrow, mrow), "dv")
                W(v, "dv")
                I(v.tensor_sub(qrow, qrow, mm2), "dv")
                fence_dv(v)
            DV_VAR = ctr.val("dv")

            @block.scalar
            def _(s):
                W(s, "dv", DV_VAR)
                W(s, "ac")
                I(s.activation(mm2, qrow, AF.Ln,
                               bias=actc_s[0:1, AC_EPS:AC_EPS + 1]), "ac")
                W(s, "ac")
                I(s.activation(qrow, mm2, AF.Exp,
                               scale=actc_s[0:1, AC_NEGH:AC_NEGH + 1]), "ac")
                fence_ac(s)
            AC_S = ctr.val("ac")

            @block.vector
            def _(v):
                W(v, "ac", AC_S)
                W(v, "dv")
                I(v.tensor_scalar(rp[0:1, 0:512], mrow, INV128, None, ALU.mult),
                  "dv")
                W(v, "dv")
                I(v.tensor_scalar(rp[0:1, 512:1024], qrow, 128.0, None, ALU.mult),
                  "dv")
                fence_dv(v)
            DV_RP = ctr.val("dv")

            @block.tensor
            def _(t):
                W(t, "dv", DV_RP)
                I(t.matmul(pStat[:, 0:512], on16[:, :], rp[:, 0:512],
                           start=True, stop=True), "pe")
                I(t.matmul(pStat[:, 512:1024], on16[:, :], rp[:, 512:1024],
                           start=True, stop=True), "pe")
            PE_BS = ctr.val("pe")

            @block.vector
            def _(v):
                W(v, "pe", PE_BS)
                I(v.tensor_copy(mb_out[:, :], pStat[:, 0:512]), "dv")
                W(v, "dv")
                I(v.tensor_copy(sb_out[:, :], pStat[:, 512:1024]), "dv")
                if not ms_only:
                    for i in range(NT):
                        I(v.scalar_tensor_tensor(xct(i), xt(i, w), INV128,
                                                 mb_out[:, :],
                                                 ALU.mult, ALU.subtract), "dv")
                fence_dv(v)
            return ctr.val("dv")

        def lora_traw(dv_xc):
            @block.tensor
            def _(t):
                W(t, "dv", dv_xc)
                for i in range(NT):
                    I(t.matmul(pStat[0:16, 0:512], wa16[:, i * 16:(i + 1) * 16],
                               xct(i), start=(i == 0), stop=(i == NT - 1)), "pe")
            PE_TR = ctr.val("pe")

            @block.vector
            def _(v):
                W(v, "pe", PE_TR)
                I(v.tensor_copy(traw16[0:16, :], pStat[0:16, 0:512]), "dv")
                fence_dv(v)
            return ctr.val("dv")

        def ffn_slot(w, o, mile, dv_ready, final=False, prev_mm=0, prev_st=0,
                     wq="sync"):
            if True:
                bi = o % 2

                @(block.sync if wq == "sync" else block.gpsimd)
                def _(sync, o=o):
                    if o % 8 == 0 or wq != "sync":
                        W(sync, "cc", AGQ[o // 8])
                    for h2 in range(2):
                        h = 2 * o + h2
                        if h < 3:
                            W(sync, "pe", prev_mm)
                        else:
                            W(sync, "pe", mile[("mmh", h - 3)])
                        src = AP(wfull_t, o * 262144 + h2 * 1024,
                                 [[2048, 128], [1, 1024]])
                        dst = wbuf[:, (h % 3) * 2048:
                                   (h % 3 + 1) * 2048].bitcast(F32)
                        I(sync.dma_start(out=dst, in_=src), "lw%d" % (h % 3), 16)
                        mile[("ldh", h)] = ctr.val("lw%d" % (h % 3))

                @block.tensor
                def _(t, o=o, bi=bi):
                    if o == 0:
                        W(t, "dv", dv_ready)
                        W(t, "lx", LG_WB)
                    if o >= 2:
                        W(t, "ac", mile[("ev", o - 2)])
                    p = pA if bi == 0 else pC
                    for h2 in range(2):
                        h = 2 * o + h2
                        W(t, "lw%d" % (h % 3), mile[("ldh", h)])
                        hb = (h % 3) * 2048
                        for i in range(16):
                            it = h2 * 16 + i
                            mm = t.matmul(p[:, 0:512],
                                          wbuf[:, hb + i * 128:hb + (i + 1) * 128],
                                          xct(it), start=(it == 0), stop=False)
                            if i == 15 and h2 == 0:
                                I(mm, "pe")
                                mile[("mmh", h)] = ctr.val("pe")
                    ot = 4 * (o % 8) + o // 8
                    I(t.matmul(p[:, 0:512], wb16[:, ot * 128:(ot + 1) * 128],
                               traw16[:, :], start=False, stop=True), "pe")
                    mile[("mmh", 2 * o + 1)] = ctr.val("pe")
                    mile[("mm", o)] = ctr.val("pe")

                @block.scalar
                def _(s, o=o, bi=bi):
                    W(s, "pe", mile[("mm", o)])
                    if o >= 2:
                        W(s, "dv", mile[("c", o - 2)])
                    p = pA if bi == 0 else pC
                    I(s.activation(raws[:, bi * 512:(bi + 1) * 512], p[:, 0:512],
                                   AF.Copy), "ac")
                    mile[("ev", o)] = fence_ac(s)

                @block.vector
                def _(v, o=o, bi=bi):
                    W(v, "ac", mile[("ev", o)])
                    if final and o >= 2:
                        W(v, "lo%d" % bi, mile[("st", o - 2)])
                    if final and o < 2:
                        W(v, "lo%d" % bi, prev_st[bi] if prev_st else 0)
                    ot = 4 * (o % 8) + o // 8
                    rb = raws[:, bi * 512:(bi + 1) * 512]
                    cb = cbuf[:, bi * 512:(bi + 1) * 512]
                    I(v.tensor_mul(cb, rb, sbc[:, :]), "dv")
                    if not final:
                        I(v.tensor_add(xt(ot, w), cb, xt(ot, w)), "dv")
                    else:
                        ob = outst[:, bi * 512:(bi + 1) * 512]
                        I(v.tensor_add(ob, cb, xt(ot, w)), "dv")
                    mile[("c", o)] = fence_dv(v)

                if final:
                    @block.gpsimd
                    def _(g, o=o, bi=bi):
                        W(g, "dv", mile[("c", o)])
                        ob = outst[:, bi * 512:(bi + 1) * 512]
                        ot = 4 * (o % 8) + o // 8
                        I(g.dma_start(out=outT[ot * 128:(ot + 1) * 128,
                                               w * WAVE:(w + 1) * WAVE],
                                      in_=ob), "lo%d" % bi, 16)
                        mile[("st", o)] = ctr.val("lo%d" % bi)
        def ffn_end(mile, final=False):
            cft = {4 * (o % 8) + o // 8: mile[("c", o)] for o in range(NT)}
            return (ctr.val("dv"), mile[("mm", NT - 1)],
                    ((mile[("st", NT - 2)], mile[("st", NT - 1)]) if final else None),
                    cft)

        def ffn(w, dv_ready, final=False, prev_mm=0, prev_st=0):
            mile = {}
            for o in range(NT):
                ffn_slot(w, o, mile, dv_ready, final, prev_mm, prev_st)
            return ffn_end(mile, final)

        def conv2(w, dv_ready):
            mile = {}
            groups = [(ci, a) for ci in range(8) for a in range(4)]
            for gidx, (ci, a) in enumerate(groups):
                bi = gidx % 2
                ft = 4 * ci + a

                @block.tensor
                def _(t, ci=ci, a=a, gidx=gidx, bi=bi):
                    if gidx == 0:
                        W(t, "dv", dv_ready)
                        W(t, "ac", AC_T16)
                        W(t, "ac", AC_IDD)
                    if gidx >= 2:
                        W(t, "ac", mile[("ev", gidx - 2)])
                    p = pA if bi == 0 else pC
                    ft_ = 4 * ci + a
                    I(t.matmul(p[:, 0:512], idd16[:, :], xct(ft_),
                               start=True, stop=False), "pe")
                    terms = []
                    for d in range(ci, -1, -1):
                        bs = range(4) if d > 0 else range(a + 1)
                        for b_ in bs:
                            terms.append((d * 7 + (a - b_ + 3),
                                          4 * (ci - d) + b_))
                    for k_, (slot, xti) in enumerate(terms):
                        I(t.matmul(p[:, 0:512],
                                   t16[:, slot * 128:(slot + 1) * 128],
                                   xct(xti), start=False,
                                   stop=(k_ == len(terms) - 1)), "pe")
                    mile[("mm", gidx)] = ctr.val("pe")

                @block.scalar
                def _(s, gidx=gidx, bi=bi):
                    W(s, "pe", mile[("mm", gidx)])
                    if gidx >= 2:
                        W(s, "dv", mile[("c", gidx - 2)])
                    p = pA if bi == 0 else pC
                    I(s.activation(raws[:, bi * 512:(bi + 1) * 512], p[:, 0:512],
                                   AF.Copy), "ac")
                    mile[("ev", gidx)] = fence_ac(s)

                @block.vector
                def _(v, gidx=gidx, bi=bi, ft=ft):
                    W(v, "ac", mile[("ev", gidx)])
                    rb = raws[:, bi * 512:(bi + 1) * 512]
                    cb = cbuf[:, bi * 512:(bi + 1) * 512]
                    I(v.tensor_mul(cb, rb, sbc[:, :]), "dv")
                    I(v.tensor_add(xt(ft, w), cb, xt(ft, w)), "dv")
                    mile[("c", gidx)] = fence_dv(v)
            return ctr.val("dv"), {4 * ci + a: mile[("c", g2)]
                                   for g2, (ci, a) in enumerate(groups)}

        def emit_pair(k):
            kbi = k % 2

            @block.scalar
            def _(s, k=k, kbi=kbi):
                W(s, "ld%d" % kbi, deq[("ld", k)])
                if k >= 2:
                    W(s, "dv", deq[("am", k - 2)])
                z = scr[:, 3072 + kbi * 1024:4096 + kbi * 1024]
                u = scr[:, 5120 + kbi * 1024:6144 + kbi * 1024]
                I(s.activation(z, cod[:, kbi * 1024:(kbi + 1) * 1024], AF.Copy),
                  "ac")
                deq[("cast", k)] = ctr.val("ac")
                I(s.activation(u, z, AF.Square,
                               bias=actc_s[:, AC_M75:AC_M75 + 1]), "ac")
                deq[("sq", k)] = fence_ac(s)

            @block.vector
            def _(v, k=k, kbi=kbi):
                W(v, "ac", deq[("sq", k)])
                W(v, "la%d" % kbi, deq[("la", k)])
                if k >= 2:
                    W(v, "lgw%d" % kbi, deq[("st", k - 2)])
                z = scr[:, 3072 + kbi * 1024:4096 + kbi * 1024]
                wv = scr[:, 5120 + kbi * 1024:6144 + kbi * 1024]
                pev = scr[:, 7168:8192]
                pov = scr[:, 8192:9216]
                I(v.tensor_scalar(wv, wv, 1.0 / PS0, -PC0 / PS0, ALU.mult,
                                  ALU.add), "dv")
                I(v.tensor_scalar(z, z, -7.5, None, ALU.add), "dv")
                I(v.tensor_scalar(pev, wv, CE[7], None, ALU.mult), "dv")
                for j in range(6, 0, -1):
                    I(v.scalar_tensor_tensor(pev, pev, CE[j], wv, ALU.add,
                                             ALU.mult), "dv")
                I(v.tensor_scalar(pov, wv, CO[7], None, ALU.mult), "dv")
                for j in range(6, 0, -1):
                    I(v.scalar_tensor_tensor(pov, pov, CO[j], wv, ALU.add,
                                             ALU.mult), "dv")
                I(v.scalar_tensor_tensor(pov, pov, CO[0], z, ALU.add,
                                         ALU.mult), "dv")
                I(v.scalar_tensor_tensor(pev, pev, CE[0], pov, ALU.add,
                                         ALU.add), "dv")
                I(v.scalar_tensor_tensor(wout[:, kbi * 1024:(kbi + 1) * 1024],
                                         pev, 0.5,
                                         ambt[:, kbi * 1024:(kbi + 1) * 1024],
                                         ALU.mult, ALU.mult), "dv")
                deq[("am", k)] = fence_dv(v)

            if k + 2 < NT // 2:
                deq_load(k + 2)

            @block.gpsimd
            def _(g, k=k, kbi=kbi):
                W(g, "dv", deq[("am", k)])
                dst = AP(wsl_t, (k // 4) * 262144 + (k % 4) * 512,
                         [[2048, 128], [64, 8], [1, 64]])
                srcw = wout[:, kbi * 1024:(kbi + 1) * 1024].bitcast(F32)
                I(g.dma_start(out=dst, in_=srcw), "lgw%d" % kbi, 16)
                deq[("st", k)] = ctr.val("lgw%d" % kbi)

            if k % 4 == 3:
                q = k // 4

                @block.gpsimd
                def _(g, q=q, k=k):
                    W(g, "lgw0", deq[("st", k - 1)])
                    W(g, "lgw1", deq[("st", k)])
                    I(g.collective_compute(
                        "AllGather", ALU.bypass,
                        replica_groups=[list(range(NCORE))],
                        ins=[wslice.ap()[q * 262144:(q + 1) * 262144].opt()],
                        outs=[wfull.ap()[q * 2097152:(q + 1) * 2097152].opt()]), "cc")
                AGQ[q] = ctr.val("cc")

        for k in range(6):
            emit_pair(k)
        HOIST_XC0 = stats_and_xc(0, DV_VT, need_lg=LG_X)
        HOIST_TR0 = lora_traw(HOIST_XC0)
        HOIST_MS1 = stats_and_xc(1, HOIST_TR0, need_lg=LG_X, ms_only=True)
        for k in range(6, NT // 2):
            emit_pair(k)
            if k >= 7 and k % 2 == 1:
                for s3 in range(2 * (k - 7), 2 * (k - 7) + 4):
                    ffn_slot(0, s3, ffn1_mile, HOIST_MS1)
        for s3 in range(20, NT):
            ffn_slot(0, s3, ffn1_mile, HOIST_MS1, wq="gpsimd")
        FFN1W0 = ffn_end(ffn1_mile)
        FFN1_DVEND = FFN1W0[0]
        CC_DONE = ctr.val("cc")

        lamre, lamim = sm8[:, 0:1], sm8[:, 1:2]
        cau = {}
        for ck in range(8):
            bi = ck % 2

            @block.sync
            def _(sync, ck=ck):
                if ck >= 1:
                    W(sync, "dv", cau[("dv", ck - 1)])
                if ck == 0:
                    W(sync, "lgw0", deq[("st", 14)])
                    W(sync, "lgw1", deq[("st", 15)])
                I(sync.dma_start(out=wout[0:64, 0:1024].bitcast(F32),
                                 in_=tbc_d[:, ck * 512:(ck + 1) * 512]), "lt", 16)
            cau[("ld", ck)] = ctr.val("lt")

            @block.vector
            def _(v, ck=ck, bi=bi):
                W(v, "lt", cau[("ld", ck)])
                if ck == 0:
                    W(v, "dv", DV_VT)
                if ck >= 2:
                    W(v, "pe", cau[("mm", ck - 2)])
                tb = wout[0:64, 0:1024].bitcast(F32)
                dre = scr[0:64, 0:512]
                dim = scr[0:64, 512:1024]
                W(v, "dv")
                I(v.tensor_scalar(dre, tb, 0.0, lamre, ALU.mult, ALU.subtract), "dv")
                W(v, "dv")
                I(v.tensor_scalar(dim, tb, a2D[0:64, 0:1], lamim, ALU.mult,
                                  ALU.subtract), "dv")
                q1 = scr[0:64, 1024:1536]
                mag = scr[0:64, 1536:2048]
                W(v, "dv")
                I(v.tensor_mul(q1, dre, dre), "dv")
                W(v, "dv")
                I(v.tensor_mul(mag, dim, dim), "dv")
                W(v, "dv")
                I(v.tensor_add(mag, mag, q1), "dv")
                rmag = scr[0:64, 2048:2560]
                W(v, "dv")
                I(v.reciprocal(rmag, mag), "dv")
                iim = scr[0:64, 2560:3072]
                W(v, "dv")
                kv = scr[:, 3072 + (ck % 2) * 512:3072 + (ck % 2 + 1) * 512]
                W(v, "dv")
                I(v.tensor_mul(kv[0:64, :], dre, rmag), "dv")
                W(v, "dv")
                I(v.scalar_tensor_tensor(iim, dim, -1.0, rmag,
                                         ALU.mult, ALU.mult), "dv")
                W(v, "dv")
                I(v.tensor_copy(kv[64:128, :], iim), "dv")
                cau[("dv", ck)] = fence_dv(v)

            @block.tensor
            def _(t, ck=ck):
                W(t, "dv", cau[("dv", ck)])
                if ck >= 1:
                    W(t, "dv", cau.get(("ev", ck - 1), 0))
                I(t.matmul(pD[0:8, 0:512], vtile[:, :],
                           scr[:, 3072 + (ck % 2) * 512:3072 + (ck % 2 + 1) * 512],
                           start=True, stop=True), "pe")
                cau[("mm", ck)] = ctr.val("pe")

            @block.vector
            def _(v, ck=ck):
                W(v, "pe", cau[("mm", ck)])
                W(v, "dv")
                I(v.tensor_copy(han[0:8, 0:1024].bitcast(F32), pD[0:8, 0:512]),
                  "dv")
                cau[("ev", ck)] = fence_dv(v)

            @block.gpsimd
            def _(g, ck=ck):
                W(g, "dv", cau[("ev", ck)])
                I(g.dma_start(out=AP(krows_t, ck * 512, [[4096, 8], [1, 512]]),
                              in_=han[0:8, 0:1024].bitcast(F32)), "lkst", 16)
                cau[("kst", ck)] = ctr.val("lkst")
        LG_KROWS = ctr.val("lkst")
        CAU_SCR_DONE = cau[("dv", 7)]

        @block.sync
        def _(sync):
            W(sync, "lkst", LG_KROWS)
            for j in range(8):
                src = AP(krows_t, j * 4096, [[64, 64], [1, 64]])
                I(sync.dma_start(out=scr[0:64, j * 64:(j + 1) * 64], in_=src),
                  "lp", 16)
        LS_PLANES = ctr.val("lp")

        if "k" in dbg:
            @block.gpsimd
            def _(g):
                W(g, "lp", LS_PLANES)
                I(g.dma_start(out=dbg_t["dbg_cau"], in_=scr[0:64, 0:512]), "ldb", 16)
        LG_CAU = ctr.val("ldb")

        @block.vector
        def _(v):
            W(v, "lp", LS_PLANES)
            if "k" in dbg:
                W(v, "ldb", LG_CAU)
            s_ = lambda j: scr[0:64, j * 64:(j + 1) * 64]
            k00r, k00i, k01r, k01i, k10r, k10i, k11r, k11i = [s_(j) for j in range(8)]
            wr_ = s_(8)
            wi_ = s_(9)
            tq = s_(10)
            zr = s_(11)
            zi = k11i
            rzm = s_(12)
            W(v, "dv")
            I(v.tensor_mul(wr_, k01r, k10r), "dv")
            W(v, "dv")
            I(v.tensor_mul(tq, k01i, k10i), "dv")
            W(v, "dv")
            I(v.tensor_sub(wr_, wr_, tq), "dv")
            W(v, "dv")
            I(v.tensor_mul(wi_, k01r, k10i), "dv")
            W(v, "dv")
            I(v.tensor_mul(tq, k01i, k10r), "dv")
            W(v, "dv")
            I(v.tensor_add(wi_, wi_, tq), "dv")
            W(v, "dv")
            I(v.tensor_scalar(zr, k11r, 1.0, None, ALU.add), "dv")
            W(v, "dv")
            I(v.tensor_mul(tq, zr, zr), "dv")
            W(v, "dv")
            I(v.tensor_mul(rzm, zi, zi), "dv")
            W(v, "dv")
            I(v.tensor_add(tq, tq, rzm), "dv")
            W(v, "dv")
            I(v.reciprocal(rzm, tq), "dv")
            cr, ci = k01r, k01i
            W(v, "dv")
            I(v.tensor_mul(tq, wr_, zr), "dv")
            W(v, "dv")
            I(v.tensor_mul(cr, wi_, zi), "dv")
            W(v, "dv")
            I(v.tensor_add(tq, tq, cr), "dv")
            W(v, "dv")
            I(v.tensor_mul(cr, tq, rzm), "dv")
            W(v, "dv")
            I(v.tensor_mul(tq, wi_, zr), "dv")
            W(v, "dv")
            I(v.tensor_mul(ci, wr_, zi), "dv")
            W(v, "dv")
            I(v.tensor_sub(tq, tq, ci), "dv")
            W(v, "dv")
            I(v.tensor_mul(ci, tq, rzm), "dv")
            gr, gi = k00r, k00i
            W(v, "dv")
            I(v.tensor_sub(gr, k00r, cr), "dv")
            W(v, "dv")
            I(v.tensor_sub(gi, k00i, ci), "dv")
            W(v, "dv")
            I(v.tensor_mul(tq, t64_s[:, :], gi), "dv")
            W(v, "dv")
            I(v.tensor_sub(arre[0:64, :], gr, tq), "dv")
            W(v, "dv")
            I(v.tensor_mul(tq, t64_s[:, :], gr), "dv")
            W(v, "dv")
            I(v.tensor_add(arim[0:64, :], gi, tq), "dv")
            fence_dv(v)
        DV_AR = ctr.val("dv")

        @block.tensor
        def _(t):
            W(t, "dv", DV_AR)
            W(t, "dv", FFN1_DVEND)
            I(t.matmul(pA[0:64, 0:64], e64c_s[:, :], arre[:, :],
                       start=True, stop=False), "pe")
            I(t.matmul(pA[0:64, 0:64], e64sn_s[:, :], arim[:, :],
                       start=False, stop=True), "pe")
            I(t.matmul(pB[0:64, 0:64], e64c_s[:, :], arim[:, :],
                       start=True, stop=False), "pe")
            I(t.matmul(pB[0:64, 0:64], e64s_s[:, :], arre[:, :],
                       start=False, stop=True), "pe")
        PE_S = ctr.val("pe")

        @block.vector
        def _(v):
            W(v, "pe", PE_S)
            sre = pA[0:64, 0:64]
            sim = pB[0:64, 0:64]
            tq1 = idt[0:64, 128:192]
            W(v, "dv")
            I(v.tensor_mul(idt[0:64, 0:64], sre, twc_s[:, :]), "dv")
            W(v, "dv")
            I(v.tensor_mul(tq1, sim, tws_s[:, :]), "dv")
            W(v, "dv")
            I(v.tensor_sub(idt[0:64, 0:64], idt[0:64, 0:64], tq1), "dv")
            W(v, "dv")
            I(v.tensor_mul(idt[0:64, 64:128], sre, tws_s[:, :]), "dv")
            W(v, "dv")
            I(v.tensor_mul(tq1, sim, twc_s[:, :]), "dv")
            W(v, "dv")
            I(v.tensor_add(idt[0:64, 64:128], idt[0:64, 64:128], tq1), "dv")
            fence_dv(v)
        DV_T2 = ctr.val("dv")

        @block.tensor
        def _(t):
            W(t, "dv", DV_T2)
            I(t.transpose(pA[0:64, 0:128], idt[:, 0:64], id32[:, :]), "pe")
            I(t.transpose(pB[0:64, 0:128], idt[:, 64:128], id32[:, :]), "pe")
        PE_TT = ctr.val("pe")

        @block.vector
        def _(v):
            W(v, "pe", PE_TT)
            W(v, "dv")
            I(v.tensor_copy(idtT[0:64, 0:64], pA[0:64, 0:64]), "dv")
            W(v, "dv")
            I(v.tensor_copy(idtT[0:64, 64:128], pB[0:64, 0:64]), "dv")
            fence_dv(v)
        DV_T2T = ctr.val("dv")

        @block.tensor
        def _(t):
            W(t, "dv", DV_T2T)
            I(t.matmul(pC[0:64, 0:64], e64cL_s[:, :], idtT[:, 0:64],
                       start=True, stop=False), "pe")
            I(t.matmul(pC[0:64, 0:64], e64snL_s[:, :], idtT[:, 64:128],
                       start=False, stop=True), "pe")
        PE_K = ctr.val("pe")

        @block.vector
        def _(v):
            W(v, "pe", PE_K)
            W(v, "dv")
            I(v.tensor_copy(k16t[:, :], pC[0:64, 0:64]), "dv")
            fence_dv(v)
        DV_K16 = ctr.val("dv")

        @block.gpsimd
        def _(g):
            W(g, "dv", DV_K16)
            W(g, "lkst", LG_KROWS)
            I(g.dma_start(out=AP(kpad_t, 0, [[1, 1], [1, 511]]),
                          in_=zrow[0:1, 0:511]), "lk", 16)
            I(g.dma_start(out=AP(kpad_t, 4607, [[1, 1], [1, 257]]),
                          in_=zrow[0:1, 0:257]), "lk", 16)
            I(g.dma_start(out=AP(kpad_t, 511, [[64, 64], [1, 64]]),
                          in_=k16t[:, :]), "lk", 16)
        LG_KPAD = ctr.val("lk")

        def emit_tbuild():
         tb_m = {}
         for d in range(8):
            bi = d % 2

            @block.sync
            def _(sync, d=d, bi=bi):
                if d == 0:
                    W(sync, "lk", LG_KPAD)
                if d >= 2:
                    W(sync, "pe", tb_m[("mm", d - 2)])
                src = AP(kpad_t, 512 * d, [[1, 128], [128, 7], [1, 128]])
                I(sync.dma_start(out=han[:, bi * 896:(bi + 1) * 896], in_=src),
                  "lh%d" % bi, 16)
            tb_m[("ld", d)] = ctr.val("lh%d" % bi)

            @block.tensor
            def _(t, d=d, bi=bi):
                W(t, "lh%d" % bi, tb_m[("ld", d)])
                if d >= 1:
                    W(t, "ac", tb_m[("ev", d - 1)])
                p1 = pB[:, 0:512] if bi == 0 else pE[:, 0:512]
                p2 = pD[:, 0:512] if bi == 0 else pB[:, 0:512]
                I(t.matmul(p1[:, 0:512], jm16[:, :],
                           han[:, bi * 896:bi * 896 + 512],
                           start=True, stop=True), "pe")
                I(t.matmul(p2[:, 0:384], jm16[:, :],
                           han[:, bi * 896 + 512:(bi + 1) * 896],
                           start=True, stop=True), "pe")
                tb_m[("mm", d)] = ctr.val("pe")

            @block.scalar
            def _(s, d=d, bi=bi):
                W(s, "pe", tb_m[("mm", d)])
                p1 = pB[:, 0:512] if bi == 0 else pE[:, 0:512]
                p2 = pD[:, 0:512] if bi == 0 else pB[:, 0:512]
                I(s.activation(t16[:, d * 896:d * 896 + 512], p1[:, 0:512],
                               AF.Copy), "ac")
                I(s.activation(t16[:, d * 896 + 512:(d + 1) * 896], p2[:, 0:384],
                               AF.Copy), "ac")
                tb_m[("ev", d)] = fence_ac(s)
         return ctr.val("ac")

        AC_T16 = emit_tbuild()

        prev_dv = None
        prev_cft = None
        last_mm = 0
        last_st = None
        for w in range(NW):
            if w == 0:
                dv_f1, last_mm, _, cft1 = FFN1W0
            else:
                @block.vector
                def _(v):
                    W(v, "dv", prev_dv)
                    W(v, "dv", HOIST_MS1)
                    I(v.tensor_copy(sbc[:, :], sbc1[:, :]), "dv")
                    for i in range(NT):
                        I(v.scalar_tensor_tensor(xct(i), xt(i, w), INV128,
                                                 mbc1[:, :],
                                                 ALU.mult, ALU.subtract), "dv")
                    fence_dv(v)
                dv_xc = ctr.val("dv")
                dv_tr = lora_traw(dv_xc)
                dv_f1, last_mm, _, cft1 = ffn(w, dv_tr, final=False,
                                              prev_mm=last_mm)
            if stop_after == "s1":
                prev_dv = dv_f1
                prev_cft = cft1
                continue
            dv_xc2 = stats_and_xc(w, dv_f1, prev_c=cft1)
            dv_c, cftc = conv2(w, dv_xc2)
            if stop_after == "s2":
                prev_dv = dv_c
                prev_cft = cftc
                continue
            dv_xc3 = stats_and_xc(w, dv_c, prev_c=cftc)
            dv_tr3 = lora_traw(dv_xc3)
            prev_dv, last_mm, last_st, prev_cft = ffn(w, dv_tr3, final=True,
                                                      prev_mm=last_mm,
                                                      prev_st=last_st)
        @block.gpsimd
        def _(g):
            if "k" in dbg:
                W(g, "ac", AC_T16)
                I(g.dma_start(out=scr[0:1, 0:2432].bitcast(F16),
                              in_=AP(kpad_t, 0, [[1, 1], [1, 4864]])), "ldb", 16)
                W(g, "ldb", ctr.val("ldb"))
                I(g.dma_start(out=AP(dbg_t["dbg_k"].tensor, 0, [[1, 1], [1, 4864]]),
                              in_=scr[0:1, 0:2432].bitcast(F16)), "ldb", 16)
                I(g.dma_start(out=dbg_t["dbg_t0"], in_=t16[:, 3 * 128:4 * 128]), "ldb", 16)
                I(g.dma_start(out=dbg_t["dbg_vt"], in_=vtile[:, :]), "ldb", 16)
                I(g.dma_start(out=dbg_t["dbg_kc"], in_=kcols[:, :]), "ldb", 16)
                I(g.dma_start(out=dbg_t["dbg_a2",], in_=a2D[:, :]), "ldb", 16)
                I(g.dma_start(out=dbg_t["dbg_kinv"], in_=scr[:, 3072:3584]), "ldb", 16)
                I(g.dma_start(out=dbg_t["dbg_ar"][:, 0:64], in_=arre[:, :]), "ldb", 16)
                I(g.dma_start(out=dbg_t["dbg_ar"][:, 64:128], in_=arim[:, :]), "ldb", 16)
            if "x" in dbg:
                W(g, "dv", prev_dv)
                for i in range(NT):
                    I(g.dma_start(out=dbg_t["dbg_x"][i * 128:(i + 1) * 128, :],
                                  in_=xt(i)), "ldb", 16)

        @block.gpsimd
        def _(g):
            for n in ["lc", "lx", "ld0", "ld1", "lgw0", "lgw1", "lt", "lp",
                      "lh0", "lh1", "lw0", "lw1", "lw2", "lk", "lkst",
                      "lo0", "lo1", "ldb"]:
                W(g, n, ctr.val(n))

    return nc


def kernel(**inputs):
    nc = build_program()
    in_maps = stage_inputs(inputs)
    res = run_bass_kernel_spmd(nc, in_maps, core_ids=list(range(NCORE)))
    parts = [np.asarray(r["outT"], dtype=np.float32) for r in res.results]
    full = np.concatenate(parts, axis=1)
    return np.ascontiguousarray(full.T).reshape(2, 4096, 4096).astype(np.float32)


# version 15 (final)
# speedup vs baseline: 1.0560x; 1.0053x over previous
import numpy as np
import ml_dtypes
import concourse.bass as bass
import concourse.mybir as mybir
from concourse.bass import AP
from concourse.bass_utils import run_bass_kernel_spmd

F32 = mybir.dt.float32
F16 = mybir.dt.float16
BF16 = mybir.dt.bfloat16
U8 = mybir.dt.uint8
AF = mybir.ActivationFunctionType
ALU = mybir.AluOpType
X = mybir.AxisListType.X

NF4 = np.array([-1.0, -0.6961928009986877, -0.5250730514526367, -0.39491748809814453,
                -0.28444138169288635, -0.18477343022823334, -0.09105003625154495, 0.0,
                0.07958029955625534, 0.16093020141124725, 0.24611230194568634,
                0.33791524171829224, 0.44070982933044434, 0.5626170039176941,
                0.7229568362236023, 1.0], dtype=np.float64)

NCORE = 8
RPC = 1024
WAVE = 512
NW = RPC // WAVE
NT = 32
OSL = 512
PC0, PS0 = 28.25, 28.0
INV128 = 1.0 / 128.0
AC_M75, AC_LN2, AC_NEG1, AC_NEGH, AC_EPS, AC_I4096, AC_HALF, AC_128 = range(8)


def nf4_coeffs():
    k = np.arange(16, dtype=np.float64)
    z = k - 7.5
    u = z * z
    uv = np.unique(u)
    even = 0.5 * (NF4 + NF4[::-1])
    odd = 0.5 * (NF4 - NF4[::-1]) / z
    idx = [np.where(u == v)[0][0] for v in uv]
    w = (uv - PC0) / PS0
    V = np.vander(w, 8, increasing=True)
    return ([float(v) for v in np.linalg.solve(V, even[idx])],
            [float(v) for v in np.linalg.solve(V, odd[idx])])


CE, CO = nf4_coeffs()

TSLOTS = []
TSLOT_OF = {}
for d in range(8):
    for a in range(4):
        for b in range(4):
            if d == 0 and b > a:
                continue
            TSLOT_OF[(d, a, b)] = len(TSLOTS)
            TSLOTS.append((d, a, b))
assert len(TSLOTS) == 122


def host_constants():
    f = np.float32
    c = {}
    jm = np.zeros((128, 128), np.float16)
    jm[np.arange(128), 127 - np.arange(128)] = 1.0
    c["jmat"] = jm
    c["ones16"] = np.ones((128, 128), np.float16)
    c["onescol16"] = np.ones((128, 1), np.float16)
    c["onescolbf"] = np.ones((128, 1), ml_dtypes.bfloat16)
    c["ident32"] = np.eye(128, dtype=f)
    ang64 = 2 * np.pi * np.outer(np.arange(64), np.arange(64)) / 64
    e64c = np.cos(ang64)
    e64s = np.sin(ang64)

    def pad(m, scale=1.0):
        out = np.zeros((128, 64), f)
        out[:64, :] = (m * scale).astype(f)
        return out

    c["e64c_p"] = pad(e64c)
    c["e64s_p"] = pad(e64s)
    c["e64sn_p"] = pad(-e64s)
    c["e64cL_p"] = pad(e64c, 1.0 / 4096.0)
    c["e64snL_p"] = pad(-e64s, 1.0 / 4096.0)
    angT = 2 * np.pi * np.outer(np.arange(64), np.arange(64)) / 4096
    c["twc"] = np.cos(angT).astype(f)
    c["tws"] = np.sin(angT).astype(f)
    t = np.tan(np.pi * np.arange(4096) / 4096)
    c["t64"] = t.reshape(64, 64).astype(f)
    c["tbc"] = np.tile(t.astype(f)[None, :], (64, 1))
    ac = np.zeros((128, 8), f)
    ac[:, AC_M75] = -7.5
    ac[:, AC_LN2] = np.log(2.0)
    ac[:, AC_NEG1] = -1.0
    ac[:, AC_NEGH] = -0.5
    ac[:, AC_EPS] = 1e-5
    ac[:, AC_I4096] = 1.0 / 4096.0
    ac[:, AC_HALF] = 0.5
    ac[:, AC_128] = 128.0
    c["actc"] = ac
    return c


def stage_inputs(inputs):
    consts = host_constants()
    x = np.ascontiguousarray(inputs["L0_attn_u"].reshape(8192, 4096).T)
    codes_u8 = np.ascontiguousarray(inputs["L0_ffn_w_codes"].view(np.uint8)[:, 0::4])
    codesT = np.ascontiguousarray(codes_u8.T)
    amB = np.repeat(np.ascontiguousarray(inputs["L0_ffn_absmax"].T), 64, axis=0)

    def omaj(cs):
        t = cs.reshape(4, 8, 128, 4, 128).transpose(3, 0, 2, 1, 4)
        return np.ascontiguousarray(t.reshape(2048, 1024))
    small8 = np.stack([
        inputs["L0_attn_Lambda_ri"][:, 0], inputs["L0_attn_Lambda_ri"][:, 1],
        inputs["L0_attn_p_ri"][:, 0], inputs["L0_attn_p_ri"][:, 1],
        inputs["L0_attn_q_ri"][:, 0], inputs["L0_attn_q_ri"][:, 1],
        inputs["L0_attn_Ct_ri"][0, :, 0], inputs["L0_attn_Ct_ri"][0, :, 1],
    ], axis=1).astype(np.float32)
    in_maps = []
    for cid in range(NCORE):
        m = dict(consts)
        m["xT"] = np.ascontiguousarray(x[:, cid * RPC:(cid + 1) * RPC])
        m["codesT"] = omaj(codesT[:, cid * OSL:(cid + 1) * OSL])
        m["amB"] = omaj(amB[:, cid * OSL:(cid + 1) * OSL].astype(np.float32))
        m["waf"] = np.ascontiguousarray(inputs["L0_ffn_W_a"].astype(np.float32))
        m["wbf"] = np.ascontiguousarray(inputs["L0_ffn_W_b"].astype(np.float32))
        m["small8"] = small8
        m["vcre"] = np.ascontiguousarray(inputs["L0_attn_Vc_ri"][:, :, 0]).astype(np.float32)
        m["vcim"] = np.ascontiguousarray(inputs["L0_attn_Vc_ri"][:, :, 1]).astype(np.float32)
        m["bpb"] = np.tile(inputs["L0_attn_Bp"].reshape(1, 64), (64, 1)).astype(np.float32)
        m["dls"] = np.array([[float(inputs["L0_attn_D"][0]),
                              float(inputs["L0_attn_log_step"][0])]], dtype=np.float32)
        in_maps.append(m)
    return in_maps


class Ctr:
    def __init__(self):
        self.c = {}

    def inc(self, name, n=1):
        self.c[name] = self.c.get(name, 0) + n
        return self.c[name]

    def val(self, name):
        return self.c.get(name, 0)


def build_program(stop_after=None, dbg=()):
    nc = bass.Bass("TRN2", target_bir_lowering=False, debug=False)
    ctr = Ctr()

    din = lambda n, s, d: nc.dram_tensor(n, s, d, kind="ExternalInput").ap()
    xT = din("xT", [4096, RPC], F32)
    codesT = din("codesT", [2048, 1024], U8)
    amB = din("amB", [2048, 1024], F32)
    waf = din("waf", [4096, 16], F32)
    wbf = din("wbf", [16, 4096], F32)
    small8 = din("small8", [64, 8], F32)
    vcre = din("vcre", [64, 64], F32)
    vcim = din("vcim", [64, 64], F32)
    bpb = din("bpb", [64, 64], F32)
    dls = din("dls", [1, 2], F32)
    jmat = din("jmat", [128, 128], F16)
    ones16d = din("ones16", [128, 128], F16)
    onescol16 = din("onescol16", [128, 1], F16)
    onescolbf = din("onescolbf", [128, 1], BF16)
    ident32d = din("ident32", [128, 128], F32)
    e64c_p = din("e64c_p", [128, 64], F32)
    e64s_p = din("e64s_p", [128, 64], F32)
    e64sn_p = din("e64sn_p", [128, 64], F32)
    e64cL_p = din("e64cL_p", [128, 64], F32)
    e64snL_p = din("e64snL_p", [128, 64], F32)
    twc_d = din("twc", [64, 64], F32)
    tws_d = din("tws", [64, 64], F32)
    t64_d = din("t64", [64, 64], F32)
    tbc_d = din("tbc", [64, 4096], F32)
    actc_d = din("actc", [128, 8], F32)

    outT = nc.dram_tensor("outT", [4096, RPC], F32, kind="ExternalOutput").ap()

    wslice = nc.dram_tensor("wslice", [4 * NT * 128 * 64], F32, kind="Internal")
    wfull = nc.dram_tensor("wfull", [32 * NT * 128 * 64], F32, kind="Internal",
                           addr_space="Shared")
    kpad = nc.dram_tensor("kpad", [4864], F16, kind="Internal")
    krowsd = nc.dram_tensor("krowsd", [8 * 4096], F32, kind="Internal")

    dbg_t = {}
    if "k" in dbg:
        dbg_t["dbg_k"] = nc.dram_tensor("dbg_k", [4864], F16, kind="ExternalOutput").ap()
        dbg_t["dbg_t0"] = nc.dram_tensor("dbg_t0", [128, 128], F16, kind="ExternalOutput").ap()
        dbg_t["dbg_ar"] = nc.dram_tensor("dbg_ar", [128, 128], F32, kind="ExternalOutput").ap()
        dbg_t["dbg_cau"] = nc.dram_tensor("dbg_cau", [64, 512], F32, kind="ExternalOutput").ap()
        dbg_t["dbg_vt"] = nc.dram_tensor("dbg_vt", [128, 8], F32, kind="ExternalOutput").ap()
        dbg_t["dbg_kc"] = nc.dram_tensor("dbg_kc", [64, 16], F32, kind="ExternalOutput").ap()
        dbg_t["dbg_a2", ] = nc.dram_tensor("dbg_a2", [128, 4], F32, kind="ExternalOutput").ap()
        dbg_t["dbg_kinv"] = nc.dram_tensor("dbg_kinv", [128, 512], F32, kind="ExternalOutput").ap()
    if "x" in dbg:
        dbg_t["dbg_x"] = nc.dram_tensor("dbg_x", [4096, RPC], F16, kind="ExternalOutput").ap()
    if "w" in dbg:
        dbg_t["dbg_w"] = nc.dram_tensor("dbg_w", [4 * NT * 128 * 64], F32,
                                        kind="ExternalOutput").ap()

    from contextlib import ExitStack
    es = ExitStack()
    sbt = lambda name, shape, dtype: es.enter_context(nc.sbuf_tensor(name, shape, dtype))
    pst = lambda name, shape: es.enter_context(nc.psum_tensor(name, shape, F32))

    x16 = sbt("x16", [128, NT * RPC], F16)
    xc16 = sbt("xc16", [128, NT * WAVE], F16)
    t16 = sbt("t16", [128, 56 * 128], F16)
    wb16 = sbt("wb16", [128, 4096], F16)
    wa16 = sbt("wa16", [128, NT * 16], F16)
    wbuf = sbt("wbuf", [128, 3 * 2048], F16)
    traw16 = sbt("traw16", [128, WAVE], F16)
    mbc = sbt("mbc", [128, WAVE], F16)
    sbc = sbt("sbc", [128, WAVE], F16)
    mbc1 = sbt("mbc1", [128, WAVE], F16)
    sbc1 = sbt("sbc1", [128, WAVE], F16)
    rp = sbt("rp", [128, 2 * WAVE], F16)
    cod = sbt("cod", [128, 2 * 1024], U8)
    ambt = sbt("ambt", [128, 2 * 1024], F16)
    scr = sbt("scr", [128, 18 * 512], F32)
    wout = sbt("wout", [128, 2 * 1024], F16)
    xsq = sbt("xsq", [128, 2 * 512], BF16)
    raws = sbt("raws", [128, 2 * 512], F16)
    cbuf = sbt("cbuf", [128, 2 * 512], F16)
    outst = sbt("outst", [128, 2 * 512], F32)
    jm16 = sbt("jm16", [128, 128], F16)
    idd16 = sbt("idd16", [128, 128], F16)
    on16 = sbt("on16", [128, 128], F16)
    onc16 = sbt("onc16", [128, 1], F16)
    oncbf = sbt("oncbf", [128, 1], BF16)
    id32 = sbt("id32", [128, 128], F32)
    e64c_s = sbt("e64c_s", [128, 64], F32)
    e64s_s = sbt("e64s_s", [128, 64], F32)
    e64sn_s = sbt("e64sn_s", [128, 64], F32)
    e64cL_s = sbt("e64cL_s", [128, 64], F32)
    e64snL_s = sbt("e64snL_s", [128, 64], F32)
    twc_s = sbt("twc_s", [64, 64], F32)
    tws_s = sbt("tws_s", [64, 64], F32)
    t64_s = sbt("t64_s", [64, 64], F32)
    actc_s = sbt("actc_s", [128, 8], F32)
    sm8 = sbt("sm8", [64, 8], F32)
    vcre_s = sbt("vcre_s", [64, 64], F32)
    vcim_s = sbt("vcim_s", [64, 64], F32)
    bpb_s = sbt("bpb_s", [64, 64], F32)
    dls_s = sbt("dls_s", [1, 2], F32)
    kcols = sbt("kcols", [64, 16], F32)
    vtile = sbt("vtile", [128, 8], F32)
    acol = sbt("acol", [128, 4], F32)
    a2D = sbt("a2D", [128, 4], F32)
    ones32 = sbt("ones32", [128, 128], F32)
    rowsc = sbt("rowsc", [1, 16], F32)
    arre = sbt("arre", [128, 64], F32)
    arim = sbt("arim", [128, 64], F32)
    idt = sbt("idt", [128, 3 * 64], F32)
    idtT = sbt("idtT", [128, 2 * 64], F32)
    k16t = sbt("k16t", [64, 64], F16)
    zrow = sbt("zrow", [1, 511], F16)
    han = sbt("han", [128, 2 * 896], F16)

    pA = pst("pA", [128, 512])
    pB = pst("pB", [128, 512])
    pC = pst("pC", [128, 512])
    pStat = pst("pStat", [128, 1024])
    pD = pst("pD", [128, 512])
    pE = pst("pE", [128, 512])

    sems = {n: es.enter_context(nc.semaphore(n))
            for n in ["lc", "lx", "ld0", "ld1", "lgw0", "lgw1", "lt", "lp",
                      "lh0", "lh1", "lw0", "lw1", "lw2", "lk", "lkst",
                      "lo0", "lo1", "ldb", "la0", "la1", "pe", "dv", "ac", "cc"]}
    block = es.enter_context(nc.Block())

    def W(eng, name, val=None):
        eng.wait_ge(sems[name], ctr.val(name) if val is None else val)

    def I(instr, name, n=1):
        instr.then_inc(sems[name], n)
        return ctr.inc(name, n)

    def fence_dv(v):
        W(v, "dv")
        return I(v.tensor_copy(rowsc[0:1, 8:9], rowsc[0:1, 9:10]), "dv")

    def fence_ac(s):
        W(s, "ac")
        return I(s.activation(rowsc[0:1, 10:11], rowsc[0:1, 11:12], AF.Copy), "ac")

    def xt(i, w=None):
        if w is None:
            return x16[:, i * RPC:(i + 1) * RPC]
        return x16[:, i * RPC + w * WAVE: i * RPC + (w + 1) * WAVE]

    def xct(i):
        return xc16[:, i * WAVE:(i + 1) * WAVE]

    wsl_t = wslice.ap().tensor
    wfull_t = wfull.ap().tensor
    kpad_t = kpad.ap().tensor
    krows_t = krowsd.ap().tensor

    with es:
        @block.sync
        def _(sync):
            for dst, src in [(jm16, jmat), (on16, ones16d), (onc16, onescol16),
                             (oncbf, onescolbf), (id32, ident32d),
                             (e64c_s, e64c_p), (e64s_s, e64s_p), (e64sn_s, e64sn_p),
                             (e64cL_s, e64cL_p), (e64snL_s, e64snL_p),
                             (twc_s, twc_d), (tws_s, tws_d), (t64_s, t64_d),
                             (actc_s, actc_d), (sm8, small8), (vcre_s, vcre),
                             (vcim_s, vcim), (bpb_s, bpb), (dls_s, dls)]:
                I(sync.dma_start(out=dst[:, :], in_=src), "lc", 16)
        LS_CONST = ctr.val("lc")

        deq = {}
        AGQ = {}
        ffn1_mile = {}

        def deq_load(k):
            kbi = k % 2

            @block.gpsimd
            def _(g, k=k, kbi=kbi):
                if k >= 2:
                    W(g, "ac", deq[("cast", k - 2)])
                    W(g, "dv", deq[("am", k - 2)])
                srcc = AP(codesT.tensor, k * 131072, [[1024, 128], [1, 1024]])
                I(g.dma_start(out=cod[:, kbi * 1024:(kbi + 1) * 1024], in_=srcc),
                  "ld%d" % kbi, 16)
                srca = AP(amB.tensor, k * 131072, [[1024, 128], [1, 1024]])
                I(g.dma_start(out=ambt[:, kbi * 1024:(kbi + 1) * 1024], in_=srca),
                  "la%d" % kbi, 16)
            deq[("ld", k)] = ctr.val("ld%d" % kbi)
            deq[("la", k)] = ctr.val("la%d" % kbi)
        deq_load(0)
        deq_load(1)

        @block.gpsimd
        def _(g):
            for i in range(NT):
                I(g.dma_start(out=xt(i), in_=xT[i * 128:(i + 1) * 128, :]), "lx", 16)
            for i in range(NT):
                I(g.dma_start(out=wa16[:, i * 16:(i + 1) * 16],
                              in_=waf[i * 128:(i + 1) * 128, :]), "lx", 16)
        LG_X = ctr.val("lx")

        @block.vector
        def _(v):
            W(v, "lc", LS_CONST)
            for buf in [rp, arre, arim, idtT, traw16, wb16, zrow, acol, idt,
                        rowsc]:
                W(v, "dv")
                I(v.memset(buf[:, :], 0.0), "dv")
            W(v, "dv")
            I(v.memset(ones32[:, :], 1.0), "dv")
            fence_dv(v)
        DV_MEM = ctr.val("dv")

        @block.gpsimd
        def _(g):
            W(g, "dv", DV_MEM)
            I(g.dma_start(out=wb16[0:16, :], in_=wbf), "lx", 16)
        LG_WB = ctr.val("lx")

        @block.scalar
        def _(s):
            import concourse.bass_isa as bass_isa
            s.add_instruction(mybir.InstLoadActFuncSet(
                name=nc.get_next_instruction_name() if hasattr(nc, "get_next_instruction_name") else "I-preload",
                act_func_set_id=6, ins=[], outs=[]))
            W(s, "lc", LS_CONST)
            W(s, "dv", DV_MEM)
            W(s, "ac")
            I(s.activation(rowsc[0:1, 0:1], dls_s[0:1, 1:2], AF.Exp,
                           bias=actc_s[0:1, AC_LN2:AC_LN2 + 1],
                           scale=actc_s[0:1, AC_NEG1:AC_NEG1 + 1]), "ac")
            fence_ac(s)
        AC_A2 = ctr.val("ac")

        @block.vector
        def _(v):
            W(v, "dv", DV_MEM)
            W(v, "ac", AC_A2)
            W(v, "dv")
            I(v.tensor_copy(acol[0:1, 0:1], rowsc[0:1, 0:1]), "dv")
            W(v, "dv")
            I(v.tensor_copy(acol[0:1, 1:2], dls_s[0:1, 0:1]), "dv")
            fence_dv(v)
        DV_ACOL = ctr.val("dv")

        @block.tensor
        def _(t):
            W(t, "dv", DV_ACOL)
            I(t.matmul(pC[:, 0:4], ones32[:, :], acol[:, 0:4],
                       start=True, stop=True), "pe")
        PE_A2 = ctr.val("pe")

        @block.vector
        def _(v):
            W(v, "pe", PE_A2)
            W(v, "dv")
            I(v.tensor_copy(a2D[:, :], pC[:, 0:4]), "dv")
            W(v, "dv")
            I(v.tensor_mul(scr[0:64, 0:64], vcre_s[:, :], bpb_s[:, :]), "dv")
            W(v, "dv")
            I(v.reduce_sum(kcols[:, 0:1], scr[0:64, 0:64], axis=X), "dv")
            W(v, "dv")
            I(v.tensor_mul(scr[0:64, 0:64], vcim_s[:, :], bpb_s[:, :]), "dv")
            W(v, "dv")
            I(v.reduce_sum(kcols[:, 1:2], scr[0:64, 0:64], axis=X), "dv")
            ctre, ctim = sm8[:, 6:7], sm8[:, 7:8]
            pre_, pim_ = sm8[:, 2:3], sm8[:, 3:4]
            qre_, qim_ = sm8[:, 4:5], sm8[:, 5:6]
            bre, bim = kcols[:, 0:1], kcols[:, 1:2]
            nctim = kcols[:, 14:15]
            nqim = kcols[:, 13:14]
            W(v, "dv")
            I(v.tensor_scalar(nctim, ctim, -1.0, None, ALU.mult), "dv")
            W(v, "dv")
            I(v.tensor_scalar(nqim, qim_, -1.0, None, ALU.mult), "dv")
            tmpc = kcols[:, 15:16]

            def cmulcols(ar_, ai_, br_, bi_, o_re, o_im):
                W(v, "dv")
                I(v.tensor_mul(o_re, ar_, br_), "dv")
                W(v, "dv")
                I(v.tensor_mul(tmpc, ai_, bi_), "dv")
                W(v, "dv")
                I(v.tensor_sub(o_re, o_re, tmpc), "dv")
                W(v, "dv")
                I(v.tensor_mul(o_im, ar_, bi_), "dv")
                W(v, "dv")
                I(v.tensor_mul(tmpc, ai_, br_), "dv")
                W(v, "dv")
                I(v.tensor_add(o_im, o_im, tmpc), "dv")

            cmulcols(ctre, nctim, bre, bim, kcols[:, 2:3], kcols[:, 3:4])
            cmulcols(ctre, nctim, pre_, pim_, kcols[:, 4:5], kcols[:, 5:6])
            cmulcols(qre_, nqim, bre, bim, kcols[:, 6:7], kcols[:, 7:8])
            cmulcols(qre_, nqim, pre_, pim_, kcols[:, 8:9], kcols[:, 9:10])
            for j, (rc, ic) in enumerate([(2, 3), (4, 5), (6, 7), (8, 9)]):
                vre = kcols[:, rc:rc + 1]
                vim = kcols[:, ic:ic + 1]
                W(v, "dv")
                I(v.tensor_copy(vtile[0:64, 2 * j:2 * j + 1], vre), "dv")
                W(v, "dv")
                I(v.tensor_scalar(vtile[64:128, 2 * j:2 * j + 1], vim, -1.0,
                                  None, ALU.mult), "dv")
                W(v, "dv")
                I(v.tensor_copy(vtile[0:64, 2 * j + 1:2 * j + 2], vim), "dv")
                W(v, "dv")
                I(v.tensor_copy(vtile[64:128, 2 * j + 1:2 * j + 2], vre), "dv")
            fence_dv(v)
        DV_VT = ctr.val("dv")

        @block.scalar
        def _(s):
            W(s, "dv", DV_VT)
            W(s, "ac")
            I(s.activation(idd16[:, :], id32[:, :], AF.Copy,
                           scale=a2D[:, 1:2]), "ac")
            fence_ac(s)
        AC_IDD = ctr.val("ac")

        def stats_and_xc(w, prev_dv, need_lg=None, prev_c=None, ms_only=False):
            st = {}
            mb_out = mbc1 if ms_only else mbc
            sb_out = sbc1 if ms_only else sbc

            @block.tensor
            def _(t):
                if prev_c is None:
                    W(t, "dv", prev_dv)
                if need_lg is not None:
                    W(t, "lx", need_lg)
                for i in range(NT):
                    if prev_c is not None:
                        W(t, "dv", prev_c[i])
                    I(t.matmul(pStat[0:1, 0:512], onc16[:, :], xt(i, w),
                               start=(i == 0), stop=(i == NT - 1)), "pe")

            for i in range(NT):
                bi = i % 2

                @block.scalar
                def _(s, i=i, bi=bi):
                    if i == 0:
                        if prev_c is None:
                            W(s, "dv", prev_dv)
                        if need_lg is not None:
                            W(s, "lx", need_lg)
                    if prev_c is not None:
                        W(s, "dv", prev_c[i])
                    if i >= 2:
                        W(s, "pe", st[("mm", i - 2)])
                    I(s.activation(xsq[:, bi * 512:(bi + 1) * 512], xt(i, w),
                                   AF.Square), "ac")
                    st[("sq", i)] = fence_ac(s)

                @block.tensor
                def _(t, i=i, bi=bi):
                    W(t, "ac", st[("sq", i)])
                    I(t.matmul(pStat[0:1, 512:1024], oncbf[:, :],
                               xsq[:, bi * 512:(bi + 1) * 512],
                               start=(i == 0), stop=(i == NT - 1)), "pe")
                    st[("mm", i)] = ctr.val("pe")
            PE_ST = ctr.val("pe")

            mrow = cbuf[0:1, 0:1024].bitcast(F32)
            qrow = xsq[0:1, 0:1024].bitcast(F32)
            mm2 = raws[0:1, 0:1024].bitcast(F32)

            @block.vector
            def _(v):
                W(v, "pe", PE_ST)
                W(v, "dv")
                I(v.tensor_scalar(mrow, pStat[0:1, 0:512], 1.0 / 4096.0, None,
                                  ALU.mult), "dv")
                W(v, "dv")
                I(v.tensor_scalar(qrow, pStat[0:1, 512:1024], 1.0 / 4096.0, None,
                                  ALU.mult), "dv")
                W(v, "dv")
                I(v.tensor_mul(mm2, mrow, mrow), "dv")
                W(v, "dv")
                I(v.tensor_sub(qrow, qrow, mm2), "dv")
                fence_dv(v)
            DV_VAR = ctr.val("dv")

            @block.scalar
            def _(s):
                W(s, "dv", DV_VAR)
                W(s, "ac")
                I(s.activation(mm2, qrow, AF.Ln,
                               bias=actc_s[0:1, AC_EPS:AC_EPS + 1]), "ac")
                W(s, "ac")
                I(s.activation(qrow, mm2, AF.Exp,
                               scale=actc_s[0:1, AC_NEGH:AC_NEGH + 1]), "ac")
                fence_ac(s)
            AC_S = ctr.val("ac")

            @block.vector
            def _(v):
                W(v, "ac", AC_S)
                W(v, "dv")
                I(v.tensor_scalar(rp[0:1, 0:512], mrow, INV128, None, ALU.mult),
                  "dv")
                W(v, "dv")
                I(v.tensor_scalar(rp[0:1, 512:1024], qrow, 128.0, None, ALU.mult),
                  "dv")
                fence_dv(v)
            DV_RP = ctr.val("dv")

            @block.tensor
            def _(t):
                W(t, "dv", DV_RP)
                I(t.matmul(pStat[:, 0:512], on16[:, :], rp[:, 0:512],
                           start=True, stop=True), "pe")
                I(t.matmul(pStat[:, 512:1024], on16[:, :], rp[:, 512:1024],
                           start=True, stop=True), "pe")
            PE_BS = ctr.val("pe")

            @block.vector
            def _(v):
                W(v, "pe", PE_BS)
                I(v.tensor_copy(mb_out[:, :], pStat[:, 0:512]), "dv")
                W(v, "dv")
                I(v.tensor_copy(sb_out[:, :], pStat[:, 512:1024]), "dv")
                if not ms_only:
                    for i in range(NT):
                        I(v.scalar_tensor_tensor(xct(i), xt(i, w), INV128,
                                                 mb_out[:, :],
                                                 ALU.mult, ALU.subtract), "dv")
                fence_dv(v)
            return ctr.val("dv")

        def lora_traw(dv_xc):
            @block.tensor
            def _(t):
                W(t, "dv", dv_xc)
                for i in range(NT):
                    I(t.matmul(pStat[0:16, 0:512], wa16[:, i * 16:(i + 1) * 16],
                               xct(i), start=(i == 0), stop=(i == NT - 1)), "pe")
            PE_TR = ctr.val("pe")

            @block.vector
            def _(v):
                W(v, "pe", PE_TR)
                I(v.tensor_copy(traw16[0:16, :], pStat[0:16, 0:512]), "dv")
                fence_dv(v)
            return ctr.val("dv")

        def ffn_slot(w, o, mile, dv_ready, final=False, prev_mm=0, prev_st=0,
                     wq="sync"):
            if True:
                bi = o % 2

                @(block.sync if wq == "sync" else block.gpsimd)
                def _(sync, o=o):
                    if o % 8 == 0 or wq != "sync":
                        W(sync, "cc", AGQ[o // 8])
                    for h2 in range(2):
                        h = 2 * o + h2
                        if h < 3:
                            W(sync, "pe", prev_mm)
                        else:
                            W(sync, "pe", mile[("mmh", h - 3)])
                        src = AP(wfull_t, o * 262144 + h2 * 1024,
                                 [[2048, 128], [1, 1024]])
                        dst = wbuf[:, (h % 3) * 2048:
                                   (h % 3 + 1) * 2048].bitcast(F32)
                        I(sync.dma_start(out=dst, in_=src), "lw%d" % (h % 3), 16)
                        mile[("ldh", h)] = ctr.val("lw%d" % (h % 3))

                @block.tensor
                def _(t, o=o, bi=bi):
                    if o == 0:
                        W(t, "dv", dv_ready)
                        W(t, "lx", LG_WB)
                    if o >= 2:
                        W(t, "ac", mile[("ev", o - 2)])
                    p = pA if bi == 0 else pC
                    for h2 in range(2):
                        h = 2 * o + h2
                        W(t, "lw%d" % (h % 3), mile[("ldh", h)])
                        hb = (h % 3) * 2048
                        for i in range(16):
                            it = h2 * 16 + i
                            mm = t.matmul(p[:, 0:512],
                                          wbuf[:, hb + i * 128:hb + (i + 1) * 128],
                                          xct(it), start=(it == 0), stop=False)
                            if i == 15 and h2 == 0:
                                I(mm, "pe")
                                mile[("mmh", h)] = ctr.val("pe")
                    ot = 4 * (o % 8) + o // 8
                    I(t.matmul(p[:, 0:512], wb16[:, ot * 128:(ot + 1) * 128],
                               traw16[:, :], start=False, stop=True), "pe")
                    mile[("mmh", 2 * o + 1)] = ctr.val("pe")
                    mile[("mm", o)] = ctr.val("pe")

                @block.scalar
                def _(s, o=o, bi=bi):
                    W(s, "pe", mile[("mm", o)])
                    if o >= 2:
                        W(s, "dv", mile[("c", o - 2)])
                    p = pA if bi == 0 else pC
                    I(s.activation(raws[:, bi * 512:(bi + 1) * 512], p[:, 0:512],
                                   AF.Copy), "ac")
                    mile[("ev", o)] = fence_ac(s)

                @block.vector
                def _(v, o=o, bi=bi):
                    W(v, "ac", mile[("ev", o)])
                    if final and o >= 2:
                        W(v, "lo%d" % bi, mile[("st", o - 2)])
                    if final and o < 2:
                        W(v, "lo%d" % bi, prev_st[bi] if prev_st else 0)
                    ot = 4 * (o % 8) + o // 8
                    rb = raws[:, bi * 512:(bi + 1) * 512]
                    cb = cbuf[:, bi * 512:(bi + 1) * 512]
                    I(v.tensor_mul(cb, rb, sbc[:, :]), "dv")
                    if not final:
                        I(v.tensor_add(xt(ot, w), cb, xt(ot, w)), "dv")
                    else:
                        ob = outst[:, bi * 512:(bi + 1) * 512]
                        I(v.tensor_add(ob, cb, xt(ot, w)), "dv")
                    mile[("c", o)] = fence_dv(v)

                if final:
                    @block.gpsimd
                    def _(g, o=o, bi=bi):
                        W(g, "dv", mile[("c", o)])
                        ob = outst[:, bi * 512:(bi + 1) * 512]
                        ot = 4 * (o % 8) + o // 8
                        I(g.dma_start(out=outT[ot * 128:(ot + 1) * 128,
                                               w * WAVE:(w + 1) * WAVE],
                                      in_=ob), "lo%d" % bi, 16)
                        mile[("st", o)] = ctr.val("lo%d" % bi)
        def ffn_end(mile, final=False):
            cft = {4 * (o % 8) + o // 8: mile[("c", o)] for o in range(NT)}
            return (ctr.val("dv"), mile[("mm", NT - 1)],
                    ((mile[("st", NT - 2)], mile[("st", NT - 1)]) if final else None),
                    cft)

        def ffn(w, dv_ready, final=False, prev_mm=0, prev_st=0):
            mile = {}
            for o in range(NT):
                ffn_slot(w, o, mile, dv_ready, final, prev_mm, prev_st)
            return ffn_end(mile, final)

        def conv2(w, dv_ready):
            mile = {}
            groups = [(ci, a) for ci in range(8) for a in range(4)]
            for gidx, (ci, a) in enumerate(groups):
                bi = gidx % 2
                ft = 4 * ci + a

                @block.tensor
                def _(t, ci=ci, a=a, gidx=gidx, bi=bi):
                    if gidx == 0:
                        W(t, "dv", dv_ready)
                        W(t, "ac", AC_T16)
                        W(t, "ac", AC_IDD)
                    if gidx >= 2:
                        W(t, "ac", mile[("ev", gidx - 2)])
                    p = pA if bi == 0 else pC
                    ft_ = 4 * ci + a
                    I(t.matmul(p[:, 0:512], idd16[:, :], xct(ft_),
                               start=True, stop=False), "pe")
                    terms = []
                    for d in range(ci, -1, -1):
                        bs = range(4) if d > 0 else range(a + 1)
                        for b_ in bs:
                            terms.append((d * 7 + (a - b_ + 3),
                                          4 * (ci - d) + b_))
                    for k_, (slot, xti) in enumerate(terms):
                        I(t.matmul(p[:, 0:512],
                                   t16[:, slot * 128:(slot + 1) * 128],
                                   xct(xti), start=False,
                                   stop=(k_ == len(terms) - 1)), "pe")
                    mile[("mm", gidx)] = ctr.val("pe")

                @block.scalar
                def _(s, gidx=gidx, bi=bi):
                    W(s, "pe", mile[("mm", gidx)])
                    if gidx >= 2:
                        W(s, "dv", mile[("c", gidx - 2)])
                    p = pA if bi == 0 else pC
                    I(s.activation(raws[:, bi * 512:(bi + 1) * 512], p[:, 0:512],
                                   AF.Copy), "ac")
                    mile[("ev", gidx)] = fence_ac(s)

                @block.vector
                def _(v, gidx=gidx, bi=bi, ft=ft):
                    W(v, "ac", mile[("ev", gidx)])
                    rb = raws[:, bi * 512:(bi + 1) * 512]
                    cb = cbuf[:, bi * 512:(bi + 1) * 512]
                    I(v.tensor_mul(cb, rb, sbc[:, :]), "dv")
                    I(v.tensor_add(xt(ft, w), cb, xt(ft, w)), "dv")
                    mile[("c", gidx)] = fence_dv(v)
            return ctr.val("dv"), {4 * ci + a: mile[("c", g2)]
                                   for g2, (ci, a) in enumerate(groups)}

        def emit_pair(k):
            kbi = k % 2

            @block.scalar
            def _(s, k=k, kbi=kbi):
                W(s, "ld%d" % kbi, deq[("ld", k)])
                if k >= 2:
                    W(s, "dv", deq[("am", k - 2)])
                z = scr[:, 3072 + kbi * 1024:4096 + kbi * 1024]
                u = scr[:, 5120 + kbi * 1024:6144 + kbi * 1024]
                I(s.activation(z, cod[:, kbi * 1024:(kbi + 1) * 1024], AF.Copy),
                  "ac")
                deq[("cast", k)] = ctr.val("ac")
                W(s, "ac", deq[("cast", k)])
                I(s.activation(u, z, AF.Square,
                               bias=actc_s[:, AC_M75:AC_M75 + 1]), "ac")
                deq[("sq", k)] = fence_ac(s)

            @block.vector
            def _(v, k=k, kbi=kbi):
                W(v, "ac", deq[("sq", k)])
                W(v, "la%d" % kbi, deq[("la", k)])
                if k >= 2:
                    W(v, "lgw%d" % kbi, deq[("st", k - 2)])
                z = scr[:, 3072 + kbi * 1024:4096 + kbi * 1024]
                wv = scr[:, 5120 + kbi * 1024:6144 + kbi * 1024]
                pev = scr[:, 7168:8192]
                pov = scr[:, 8192:9216]
                I(v.tensor_scalar(wv, wv, 1.0 / PS0, -PC0 / PS0, ALU.mult,
                                  ALU.add), "dv")
                I(v.tensor_scalar(z, z, -7.5, None, ALU.add), "dv")
                I(v.tensor_scalar(pev, wv, CE[7], None, ALU.mult), "dv")
                for j in range(6, 0, -1):
                    I(v.scalar_tensor_tensor(pev, pev, CE[j], wv, ALU.add,
                                             ALU.mult), "dv")
                I(v.tensor_scalar(pov, wv, CO[7], None, ALU.mult), "dv")
                for j in range(6, 0, -1):
                    I(v.scalar_tensor_tensor(pov, pov, CO[j], wv, ALU.add,
                                             ALU.mult), "dv")
                I(v.scalar_tensor_tensor(pov, pov, CO[0], z, ALU.add,
                                         ALU.mult), "dv")
                I(v.scalar_tensor_tensor(pev, pev, CE[0], pov, ALU.add,
                                         ALU.add), "dv")
                I(v.scalar_tensor_tensor(wout[:, kbi * 1024:(kbi + 1) * 1024],
                                         pev, 0.5,
                                         ambt[:, kbi * 1024:(kbi + 1) * 1024],
                                         ALU.mult, ALU.mult), "dv")
                deq[("am", k)] = fence_dv(v)

            if k + 2 < NT // 2:
                deq_load(k + 2)

            @block.gpsimd
            def _(g, k=k, kbi=kbi):
                W(g, "dv", deq[("am", k)])
                dst = AP(wsl_t, (k // 4) * 262144 + (k % 4) * 512,
                         [[2048, 128], [64, 8], [1, 64]])
                srcw = wout[:, kbi * 1024:(kbi + 1) * 1024].bitcast(F32)
                I(g.dma_start(out=dst, in_=srcw), "lgw%d" % kbi, 16)
                deq[("st", k)] = ctr.val("lgw%d" % kbi)

            if k % 4 == 3:
                q = k // 4

                @block.gpsimd
                def _(g, q=q, k=k):
                    W(g, "lgw0", deq[("st", k - 1)])
                    W(g, "lgw1", deq[("st", k)])
                    I(g.collective_compute(
                        "AllGather", ALU.bypass,
                        replica_groups=[list(range(NCORE))],
                        ins=[wslice.ap()[q * 262144:(q + 1) * 262144].opt()],
                        outs=[wfull.ap()[q * 2097152:(q + 1) * 2097152].opt()]), "cc")
                AGQ[q] = ctr.val("cc")

        for k in range(6):
            emit_pair(k)
        HOIST_XC0 = stats_and_xc(0, DV_VT, need_lg=LG_X)
        HOIST_TR0 = lora_traw(HOIST_XC0)
        HOIST_MS1 = stats_and_xc(1, HOIST_TR0, need_lg=LG_X, ms_only=True)
        for k in range(6, NT // 2):
            emit_pair(k)
            if k >= 7 and k % 2 == 1:
                for s3 in range(2 * (k - 7), 2 * (k - 7) + 4):
                    ffn_slot(0, s3, ffn1_mile, HOIST_MS1)
        for s3 in range(20, NT):
            ffn_slot(0, s3, ffn1_mile, HOIST_MS1, wq="gpsimd")
        FFN1W0 = ffn_end(ffn1_mile)
        FFN1_DVEND = FFN1W0[0]
        CC_DONE = ctr.val("cc")

        lamre, lamim = sm8[:, 0:1], sm8[:, 1:2]
        cau = {}
        for ck in range(8):
            bi = ck % 2

            @block.sync
            def _(sync, ck=ck):
                if ck >= 1:
                    W(sync, "dv", cau[("dv", ck - 1)])
                if ck == 0:
                    W(sync, "lgw0", deq[("st", 14)])
                    W(sync, "lgw1", deq[("st", 15)])
                I(sync.dma_start(out=wout[0:64, 0:1024].bitcast(F32),
                                 in_=tbc_d[:, ck * 512:(ck + 1) * 512]), "lt", 16)
            cau[("ld", ck)] = ctr.val("lt")

            @block.vector
            def _(v, ck=ck, bi=bi):
                W(v, "lt", cau[("ld", ck)])
                if ck == 0:
                    W(v, "dv", DV_VT)
                if ck >= 2:
                    W(v, "pe", cau[("mm", ck - 2)])
                tb = wout[0:64, 0:1024].bitcast(F32)
                dre = scr[0:64, 0:512]
                dim = scr[0:64, 512:1024]
                W(v, "dv")
                I(v.tensor_scalar(dre, tb, 0.0, lamre, ALU.mult, ALU.subtract), "dv")
                W(v, "dv")
                I(v.tensor_scalar(dim, tb, a2D[0:64, 0:1], lamim, ALU.mult,
                                  ALU.subtract), "dv")
                q1 = scr[0:64, 1024:1536]
                mag = scr[0:64, 1536:2048]
                W(v, "dv")
                I(v.tensor_mul(q1, dre, dre), "dv")
                W(v, "dv")
                I(v.tensor_mul(mag, dim, dim), "dv")
                W(v, "dv")
                I(v.tensor_add(mag, mag, q1), "dv")
                rmag = scr[0:64, 2048:2560]
                W(v, "dv")
                I(v.reciprocal(rmag, mag), "dv")
                iim = scr[0:64, 2560:3072]
                W(v, "dv")
                kv = scr[:, 3072 + (ck % 2) * 512:3072 + (ck % 2 + 1) * 512]
                W(v, "dv")
                I(v.tensor_mul(kv[0:64, :], dre, rmag), "dv")
                W(v, "dv")
                I(v.scalar_tensor_tensor(iim, dim, -1.0, rmag,
                                         ALU.mult, ALU.mult), "dv")
                W(v, "dv")
                I(v.tensor_copy(kv[64:128, :], iim), "dv")
                cau[("dv", ck)] = fence_dv(v)

            @block.tensor
            def _(t, ck=ck):
                W(t, "dv", cau[("dv", ck)])
                if ck >= 1:
                    W(t, "dv", cau.get(("ev", ck - 1), 0))
                I(t.matmul(pD[0:8, 0:512], vtile[:, :],
                           scr[:, 3072 + (ck % 2) * 512:3072 + (ck % 2 + 1) * 512],
                           start=True, stop=True), "pe")
                cau[("mm", ck)] = ctr.val("pe")

            @block.vector
            def _(v, ck=ck):
                W(v, "pe", cau[("mm", ck)])
                W(v, "dv")
                I(v.tensor_copy(han[0:8, 0:1024].bitcast(F32), pD[0:8, 0:512]),
                  "dv")
                cau[("ev", ck)] = fence_dv(v)

            @block.gpsimd
            def _(g, ck=ck):
                W(g, "dv", cau[("ev", ck)])
                I(g.dma_start(out=AP(krows_t, ck * 512, [[4096, 8], [1, 512]]),
                              in_=han[0:8, 0:1024].bitcast(F32)), "lkst", 16)
                cau[("kst", ck)] = ctr.val("lkst")
        LG_KROWS = ctr.val("lkst")
        CAU_SCR_DONE = cau[("dv", 7)]

        @block.sync
        def _(sync):
            W(sync, "lkst", LG_KROWS)
            for j in range(8):
                src = AP(krows_t, j * 4096, [[64, 64], [1, 64]])
                I(sync.dma_start(out=scr[0:64, j * 64:(j + 1) * 64], in_=src),
                  "lp", 16)
        LS_PLANES = ctr.val("lp")

        if "k" in dbg:
            @block.gpsimd
            def _(g):
                W(g, "lp", LS_PLANES)
                I(g.dma_start(out=dbg_t["dbg_cau"], in_=scr[0:64, 0:512]), "ldb", 16)
        LG_CAU = ctr.val("ldb")

        @block.vector
        def _(v):
            W(v, "lp", LS_PLANES)
            if "k" in dbg:
                W(v, "ldb", LG_CAU)
            s_ = lambda j: scr[0:64, j * 64:(j + 1) * 64]
            k00r, k00i, k01r, k01i, k10r, k10i, k11r, k11i = [s_(j) for j in range(8)]
            wr_ = s_(8)
            wi_ = s_(9)
            tq = s_(10)
            zr = s_(11)
            zi = k11i
            rzm = s_(12)
            W(v, "dv")
            I(v.tensor_mul(wr_, k01r, k10r), "dv")
            W(v, "dv")
            I(v.tensor_mul(tq, k01i, k10i), "dv")
            W(v, "dv")
            I(v.tensor_sub(wr_, wr_, tq), "dv")
            W(v, "dv")
            I(v.tensor_mul(wi_, k01r, k10i), "dv")
            W(v, "dv")
            I(v.tensor_mul(tq, k01i, k10r), "dv")
            W(v, "dv")
            I(v.tensor_add(wi_, wi_, tq), "dv")
            W(v, "dv")
            I(v.tensor_scalar(zr, k11r, 1.0, None, ALU.add), "dv")
            W(v, "dv")
            I(v.tensor_mul(tq, zr, zr), "dv")
            W(v, "dv")
            I(v.tensor_mul(rzm, zi, zi), "dv")
            W(v, "dv")
            I(v.tensor_add(tq, tq, rzm), "dv")
            W(v, "dv")
            I(v.reciprocal(rzm, tq), "dv")
            cr, ci = k01r, k01i
            W(v, "dv")
            I(v.tensor_mul(tq, wr_, zr), "dv")
            W(v, "dv")
            I(v.tensor_mul(cr, wi_, zi), "dv")
            W(v, "dv")
            I(v.tensor_add(tq, tq, cr), "dv")
            W(v, "dv")
            I(v.tensor_mul(cr, tq, rzm), "dv")
            W(v, "dv")
            I(v.tensor_mul(tq, wi_, zr), "dv")
            W(v, "dv")
            I(v.tensor_mul(ci, wr_, zi), "dv")
            W(v, "dv")
            I(v.tensor_sub(tq, tq, ci), "dv")
            W(v, "dv")
            I(v.tensor_mul(ci, tq, rzm), "dv")
            gr, gi = k00r, k00i
            W(v, "dv")
            I(v.tensor_sub(gr, k00r, cr), "dv")
            W(v, "dv")
            I(v.tensor_sub(gi, k00i, ci), "dv")
            W(v, "dv")
            I(v.tensor_mul(tq, t64_s[:, :], gi), "dv")
            W(v, "dv")
            I(v.tensor_sub(arre[0:64, :], gr, tq), "dv")
            W(v, "dv")
            I(v.tensor_mul(tq, t64_s[:, :], gr), "dv")
            W(v, "dv")
            I(v.tensor_add(arim[0:64, :], gi, tq), "dv")
            fence_dv(v)
        DV_AR = ctr.val("dv")

        @block.tensor
        def _(t):
            W(t, "dv", DV_AR)
            W(t, "dv", FFN1_DVEND)
            I(t.matmul(pA[0:64, 0:64], e64c_s[:, :], arre[:, :],
                       start=True, stop=False), "pe")
            I(t.matmul(pA[0:64, 0:64], e64sn_s[:, :], arim[:, :],
                       start=False, stop=True), "pe")
            I(t.matmul(pB[0:64, 0:64], e64c_s[:, :], arim[:, :],
                       start=True, stop=False), "pe")
            I(t.matmul(pB[0:64, 0:64], e64s_s[:, :], arre[:, :],
                       start=False, stop=True), "pe")
        PE_S = ctr.val("pe")

        @block.vector
        def _(v):
            W(v, "pe", PE_S)
            sre = pA[0:64, 0:64]
            sim = pB[0:64, 0:64]
            tq1 = idt[0:64, 128:192]
            W(v, "dv")
            I(v.tensor_mul(idt[0:64, 0:64], sre, twc_s[:, :]), "dv")
            W(v, "dv")
            I(v.tensor_mul(tq1, sim, tws_s[:, :]), "dv")
            W(v, "dv")
            I(v.tensor_sub(idt[0:64, 0:64], idt[0:64, 0:64], tq1), "dv")
            W(v, "dv")
            I(v.tensor_mul(idt[0:64, 64:128], sre, tws_s[:, :]), "dv")
            W(v, "dv")
            I(v.tensor_mul(tq1, sim, twc_s[:, :]), "dv")
            W(v, "dv")
            I(v.tensor_add(idt[0:64, 64:128], idt[0:64, 64:128], tq1), "dv")
            fence_dv(v)
        DV_T2 = ctr.val("dv")

        @block.tensor
        def _(t):
            W(t, "dv", DV_T2)
            I(t.transpose(pA[0:64, 0:128], idt[:, 0:64], id32[:, :]), "pe")
            I(t.transpose(pB[0:64, 0:128], idt[:, 64:128], id32[:, :]), "pe")
        PE_TT = ctr.val("pe")

        @block.vector
        def _(v):
            W(v, "pe", PE_TT)
            W(v, "dv")
            I(v.tensor_copy(idtT[0:64, 0:64], pA[0:64, 0:64]), "dv")
            W(v, "dv")
            I(v.tensor_copy(idtT[0:64, 64:128], pB[0:64, 0:64]), "dv")
            fence_dv(v)
        DV_T2T = ctr.val("dv")

        @block.tensor
        def _(t):
            W(t, "dv", DV_T2T)
            I(t.matmul(pC[0:64, 0:64], e64cL_s[:, :], idtT[:, 0:64],
                       start=True, stop=False), "pe")
            I(t.matmul(pC[0:64, 0:64], e64snL_s[:, :], idtT[:, 64:128],
                       start=False, stop=True), "pe")
        PE_K = ctr.val("pe")

        @block.vector
        def _(v):
            W(v, "pe", PE_K)
            W(v, "dv")
            I(v.tensor_copy(k16t[:, :], pC[0:64, 0:64]), "dv")
            fence_dv(v)
        DV_K16 = ctr.val("dv")

        @block.gpsimd
        def _(g):
            W(g, "dv", DV_K16)
            W(g, "lkst", LG_KROWS)
            I(g.dma_start(out=AP(kpad_t, 0, [[1, 1], [1, 511]]),
                          in_=zrow[0:1, 0:511]), "lk", 16)
            I(g.dma_start(out=AP(kpad_t, 4607, [[1, 1], [1, 257]]),
                          in_=zrow[0:1, 0:257]), "lk", 16)
            I(g.dma_start(out=AP(kpad_t, 511, [[64, 64], [1, 64]]),
                          in_=k16t[:, :]), "lk", 16)
        LG_KPAD = ctr.val("lk")

        def emit_tbuild():
         tb_m = {}
         for d in range(8):
            bi = d % 2

            @block.sync
            def _(sync, d=d, bi=bi):
                if d == 0:
                    W(sync, "lk", LG_KPAD)
                if d >= 2:
                    W(sync, "pe", tb_m[("mm", d - 2)])
                src = AP(kpad_t, 512 * d, [[1, 128], [128, 7], [1, 128]])
                I(sync.dma_start(out=han[:, bi * 896:(bi + 1) * 896], in_=src),
                  "lh%d" % bi, 16)
            tb_m[("ld", d)] = ctr.val("lh%d" % bi)

            @block.tensor
            def _(t, d=d, bi=bi):
                W(t, "lh%d" % bi, tb_m[("ld", d)])
                if d >= 1:
                    W(t, "ac", tb_m[("ev", d - 1)])
                p1 = pB[:, 0:512] if bi == 0 else pE[:, 0:512]
                p2 = pD[:, 0:512] if bi == 0 else pB[:, 0:512]
                I(t.matmul(p1[:, 0:512], jm16[:, :],
                           han[:, bi * 896:bi * 896 + 512],
                           start=True, stop=True), "pe")
                I(t.matmul(p2[:, 0:384], jm16[:, :],
                           han[:, bi * 896 + 512:(bi + 1) * 896],
                           start=True, stop=True), "pe")
                tb_m[("mm", d)] = ctr.val("pe")

            @block.scalar
            def _(s, d=d, bi=bi):
                W(s, "pe", tb_m[("mm", d)])
                p1 = pB[:, 0:512] if bi == 0 else pE[:, 0:512]
                p2 = pD[:, 0:512] if bi == 0 else pB[:, 0:512]
                I(s.activation(t16[:, d * 896:d * 896 + 512], p1[:, 0:512],
                               AF.Copy), "ac")
                I(s.activation(t16[:, d * 896 + 512:(d + 1) * 896], p2[:, 0:384],
                               AF.Copy), "ac")
                tb_m[("ev", d)] = fence_ac(s)
         return ctr.val("ac")

        AC_T16 = emit_tbuild()

        prev_dv = None
        prev_cft = None
        last_mm = 0
        last_st = None
        for w in range(NW):
            if w == 0:
                dv_f1, last_mm, _, cft1 = FFN1W0
            else:
                @block.vector
                def _(v):
                    W(v, "dv", prev_dv)
                    W(v, "dv", HOIST_MS1)
                    I(v.tensor_copy(sbc[:, :], sbc1[:, :]), "dv")
                    for i in range(NT):
                        I(v.scalar_tensor_tensor(xct(i), xt(i, w), INV128,
                                                 mbc1[:, :],
                                                 ALU.mult, ALU.subtract), "dv")
                    fence_dv(v)
                dv_xc = ctr.val("dv")
                dv_tr = lora_traw(dv_xc)
                dv_f1, last_mm, _, cft1 = ffn(w, dv_tr, final=False,
                                              prev_mm=last_mm)
            if stop_after == "s1":
                prev_dv = dv_f1
                prev_cft = cft1
                continue
            dv_xc2 = stats_and_xc(w, dv_f1, prev_c=cft1)
            dv_c, cftc = conv2(w, dv_xc2)
            if stop_after == "s2":
                prev_dv = dv_c
                prev_cft = cftc
                continue
            dv_xc3 = stats_and_xc(w, dv_c, prev_c=cftc)
            dv_tr3 = lora_traw(dv_xc3)
            prev_dv, last_mm, last_st, prev_cft = ffn(w, dv_tr3, final=True,
                                                      prev_mm=last_mm,
                                                      prev_st=last_st)
        @block.gpsimd
        def _(g):
            if "k" in dbg:
                W(g, "ac", AC_T16)
                I(g.dma_start(out=scr[0:1, 0:2432].bitcast(F16),
                              in_=AP(kpad_t, 0, [[1, 1], [1, 4864]])), "ldb", 16)
                W(g, "ldb", ctr.val("ldb"))
                I(g.dma_start(out=AP(dbg_t["dbg_k"].tensor, 0, [[1, 1], [1, 4864]]),
                              in_=scr[0:1, 0:2432].bitcast(F16)), "ldb", 16)
                I(g.dma_start(out=dbg_t["dbg_t0"], in_=t16[:, 3 * 128:4 * 128]), "ldb", 16)
                I(g.dma_start(out=dbg_t["dbg_vt"], in_=vtile[:, :]), "ldb", 16)
                I(g.dma_start(out=dbg_t["dbg_kc"], in_=kcols[:, :]), "ldb", 16)
                I(g.dma_start(out=dbg_t["dbg_a2",], in_=a2D[:, :]), "ldb", 16)
                I(g.dma_start(out=dbg_t["dbg_kinv"], in_=scr[:, 3072:3584]), "ldb", 16)
                I(g.dma_start(out=dbg_t["dbg_ar"][:, 0:64], in_=arre[:, :]), "ldb", 16)
                I(g.dma_start(out=dbg_t["dbg_ar"][:, 64:128], in_=arim[:, :]), "ldb", 16)
            if "x" in dbg:
                W(g, "dv", prev_dv)
                for i in range(NT):
                    I(g.dma_start(out=dbg_t["dbg_x"][i * 128:(i + 1) * 128, :],
                                  in_=xt(i)), "ldb", 16)

        @block.gpsimd
        def _(g):
            for n in ["lc", "lx", "ld0", "ld1", "lgw0", "lgw1", "lt", "lp",
                      "lh0", "lh1", "lw0", "lw1", "lw2", "lk", "lkst",
                      "lo0", "lo1", "ldb"]:
                W(g, n, ctr.val(n))

    return nc


def kernel(**inputs):
    nc = build_program()
    in_maps = stage_inputs(inputs)
    res = run_bass_kernel_spmd(nc, in_maps, core_ids=list(range(NCORE)))
    parts = [np.asarray(r["outT"], dtype=np.float32) for r in res.results]
    full = np.concatenate(parts, axis=1)
    return np.ascontiguousarray(full.T).reshape(2, 4096, 4096).astype(np.float32)
